# version 08
# baseline (speedup 1.0000x reference)
import contextlib
import numpy as np
import ml_dtypes
import concourse.bass as bass
import concourse.mybir as mybir
from concourse.bass_utils import run_bass_kernel_spmd

F32 = mybir.dt.float32
BF16 = mybir.dt.bfloat16
AL = mybir.AluOpType
AF = mybir.ActivationFunctionType
AX = mybir.AxisListType
bfd = ml_dtypes.bfloat16

D = 2048
T = 4096
N = 64
R = 128
H = 8192
TSC = 2048
QT = 1024
DT = 16
DHT = 8
EPS = 1e-5
MAGIC = float(np.float32(1.5 * 2 ** 23))
INVD = 1.0 / D


class Sem:
    def __init__(self, h):
        self.h = h
        self.v = 0

    def inc(self, instr, k=1):
        instr.then_inc(self.h, k)
        self.v += k
        return self.v


class Stream:

    def __init__(self, E, nc, name, nslots=2):
        self.sems = [Sem(E(nc.semaphore(f"{name}{i}"))) for i in range(nslots)]
        self.n = nslots

    def issue(self, eng, slot, out, in_, **kw):
        s = self.sems[slot % self.n]
        s.inc(eng.dma_start(out=out, in_=in_, **kw), 16)
        return (slot % self.n, s.v)

    def wait(self, eng, tok):
        slot, v = tok
        eng.wait_ge(self.sems[slot].h, v)

    def wait_all(self, eng):
        for s in self.sems:
            eng.wait_ge(s.h, s.v)


def build_program(taps=(), phases=("pre", "scan", "ffn", "post"), ffn_cut=5):
    nc = bass.Bass("TRN2", target_bir_lowering=False, debug=False)

    I = {}
    for name, shape in [
        ("xsc", [T, D]), ("xdh", [T, D // 2]), ("xq", [2 * QT, D]),
        ("nw", [D]), ("nb", [D]), ("dprm", [D]),
        ("nwdh", [D // 2]), ("nbdh", [D // 2]),
        ("alogdh", [D // 2, N]), ("dpwdh", [D // 2, R]), ("dpbdh", [D // 2]),
        ("xpw", [2 * R, D]), ("wout", [D, D]),
        ("cfc", [D, H]), ("cpr", [D, H]), ("msel", [1, 2]),
    ]:
        I[name] = nc.dram_tensor(name, shape, F32, kind="ExternalInput").ap()

    out_o = nc.dram_tensor("o", [2 * QT, D], F32, kind="ExternalOutput").ap()
    tap = {}
    for tp, shape in dict(
        xn2dh=[DHT * 128, 2 * TSC], dtraw=[R, 2 * TSC], bc=[R, 2 * TSC],
        y=[DHT * 128, 2 * TSC], g=[DT * 128, 2 * QT], vtap=[DT * 128, 2 * QT],
        sout=[DT * 128, 2 * QT], ccol=[128, 128], sc8=[128, 16],
        stats=[128, 64],
    ).items():
        if tp in taps:
            tap[tp] = nc.dram_tensor("tap_" + tp, shape, F32, kind="ExternalOutput").ap()

    d_q1 = nc.dram_tensor("d_q1", [D, H], BF16).ap()
    d_q2T = nc.dram_tensor("d_q2T", [H, D], BF16).ap()
    d_woT = nc.dram_tensor("d_woT", [16, 16, 128, 128], BF16).ap()
    d_bc = nc.dram_tensor("d_bc", [R, TSC], BF16).ap()
    d_c0 = nc.dram_tensor("d_c0", [N, TSC], BF16).ap()
    d_pk = nc.dram_tensor("d_pk", [2, QT], F32).ap()
    d_scal = nc.dram_tensor("d_scal", [1, 8], F32).ap()
    d_sq = nc.dram_tensor("d_sq", [1024, H], BF16).ap()
    d_g = nc.dram_tensor("d_g", [2, DT, 128, QT], BF16).ap()
    d_yin = [nc.dram_tensor(f"d_yin{s}", [DHT * 128, TSC], BF16) for s in range(2)]
    d_yout = [nc.dram_tensor(f"d_yout{s}", [D, TSC], BF16) for s in range(2)]

    with contextlib.ExitStack() as ctx:
        E = ctx.enter_context

        ident16 = E(nc.sbuf_tensor("ident16", [128, 128], BF16))
        a_all = E(nc.sbuf_tensor("a_all", [128, DHT * N], F32))
        ccol = E(nc.sbuf_tensor("ccol", [128, 128], F32))
        states = E(nc.sbuf_tensor("states", [128, DHT * N], F32))
        sc8 = E(nc.sbuf_tensor("sc8", [128, 16], F32))
        stat = E(nc.sbuf_tensor("stat", [128, 64], F32))
        msel_t = E(nc.sbuf_tensor("msel_t", [128, 2], F32))
        cst = E(nc.sbuf_tensor("cst", [128, 2], F32))
        jnk = E(nc.sbuf_tensor("jnk", [128, 4], F32))
        xpwT = E(nc.sbuf_tensor("xpwT", [128, 32 * 128], BF16))
        dpwT = E(nc.sbuf_tensor("dpwT", [128, DHT * 128], BF16))

        AFW = 24576
        ABW = 45056
        af = E(nc.sbuf_tensor("af", [128, AFW], F32))
        ab = E(nc.sbuf_tensor("ab", [128, ABW], BF16))

        ps = [E(nc.psum_tensor(f"ps{i}", [128, 512], F32)) for i in range(8)]

        S = {}
        for nm in ["exp", "h", "g", "y", "yev", "dl", "pe", "ve", "se", "ge", "fh", "fe",
                   "ag", "fen", "dp", "dcp", "ab2", "ln", "q"]:
            S[nm] = Sem(E(nc.semaphore(nm)))

        LV = Stream(E, nc, "lv", 1)
        LX = Stream(E, nc, "lx", 2)
        LW = Stream(E, nc, "lw", 2)
        LW2 = Stream(E, nc, "lw2", 2)
        LW3 = Stream(E, nc, "lw3", 2)
        BCS = Stream(E, nc, "bcs", 2)
        LQ = Stream(E, nc, "lq", 2)
        LQ2 = Stream(E, nc, "lq2", 2)
        LSQ = Stream(E, nc, "lsq", 2)
        LY = Stream(E, nc, "ly", 2)
        LG = Stream(E, nc, "lg", 2)
        LWO = Stream(E, nc, "lwo", 2)
        LPK = Stream(E, nc, "lpk", 1)
        SY = Stream(E, nc, "sy", 1)
        SB = Stream(E, nc, "sb", 1)
        SQS = Stream(E, nc, "sqs", 8)
        SQW = Stream(E, nc, "sqw", 1)
        SSC = Stream(E, nc, "ssc", 1)
        SG = Stream(E, nc, "sg", 1)
        SO = Stream(E, nc, "so", 1)
        SPK = Stream(E, nc, "spk", 1)
        STP = Stream(E, nc, "stp", 1)
        _ALL_STREAMS = [LV, LX, LW, LW2, LW3, BCS, LQ, LQ2, LSQ, LY, LG, LWO, LPK,
                        SY, SB, SQS, SQW, SSC, SG, SO, SPK, STP]

        block = E(nc.Block())

        def fsl(off, w):
            return af[:, off:off + w]

        def bsl(off, w):
            return ab[:, off:off + w]

        FA_X = 0
        FA_SQ = 4096
        FA_XN = 6144
        FA_XNR = 8192
        FA_DL = 10240
        FA_SP = 14336
        FA_FT = 16384
        FA_PK = 18432
        FA_MISC = 19456
        FA_W = 20480

        BA_XN2 = 0
        BA_S16 = 16384
        BA_DTR = 18432
        BA_PRE = 20480
        BA_Y16 = 20480
        BA_BCROW = 22528
        BA_BCB = 24576
        BA_BCC = 28672
        BA_DA = 32768
        BA_U = 36864
        BA_HH = 40960
        BA_FQ1 = 0
        BA_SQR = 0
        BA_FSQS = 41472
        BA_FH1 = 41984
        BA_SILO = 40960
        BA_HQT = 8192
        BA_V = 0
        BA_FIN = 16384
        BA_Y01 = 32768
        BA_SOUT = 36864
        BA_GP = 38912
        BA_WOS = 40960

        def sec(engname, fn):
            getattr(block, engname)(fn)

        def ck(eng, nm, instr):
            v = S[nm].inc(instr)
            eng.wait_ge(S[nm].h, v)

        ALL_STREAMS = _ALL_STREAMS

        def fence():
            S["fen"].v += 5
            tgt = S["fen"].v
            snap = {nm: S[nm].v for nm in S}
            stream_snap = [(st, [s.v for s in st.sems]) for st in ALL_STREAMS]

            def f_sync(eng):
                for st, vals in stream_snap:
                    for s, v in zip(st.sems, vals):
                        eng.wait_ge(s.h, v)
                eng.nop().then_inc(S["fen"].h, 1)
                eng.wait_ge(S["fen"].h, tgt)

            def f_ve(eng):
                for nm in ["ve", "h", "g", "dl", "ab2", "q", "fe"]:
                    eng.wait_ge(S[nm].h, snap[nm])
                eng.nop().then_inc(S["fen"].h, 1)
                eng.wait_ge(S["fen"].h, tgt)

            def f_se(eng):
                for nm in ["se", "exp", "dcp", "ln", "yev", "fh"]:
                    eng.wait_ge(S[nm].h, snap[nm])
                eng.nop().then_inc(S["fen"].h, 1)
                eng.wait_ge(S["fen"].h, tgt)

            def f_pe(eng):
                for nm in ["pe", "y", "dp"]:
                    eng.wait_ge(S[nm].h, snap[nm])
                eng.nop().then_inc(S["fen"].h, 1)
                eng.wait_ge(S["fen"].h, tgt)

            def f_gp(eng):
                eng.wait_ge(S["ge"].h, snap["ge"])
                eng.nop().then_inc(S["fen"].h, 1)
                eng.wait_ge(S["fen"].h, tgt)

            sec("sync", f_sync)
            sec("vector", f_ve)
            sec("scalar", f_se)
            sec("tensor", f_pe)
            sec("gpsimd", f_gp)

        def transpose16(src_off, dst_off, nblk=16):
            for r0 in range(0, nblk, 8):
                nb = min(8, nblk - r0)
                ve0 = S["ve"].v

                def f_pe(tensor, r0=r0, nb=nb, ve0=ve0):
                    tensor.wait_ge(S["ve"].h, ve0)
                    for j in range(nb):
                        tr = tensor.transpose(ps[j][:, 0:64].bitcast(BF16),
                                              bsl(src_off + (r0 + j) * 128, 128), ident16[:])
                    S["pe"].inc(tr)

                sec("tensor", f_pe)
                pe0 = S["pe"].v

                def f_ve(vector, r0=r0, nb=nb, pe0=pe0):
                    vector.wait_ge(S["pe"].h, pe0)
                    for j in range(nb):
                        S["ve"].inc(vector.tensor_copy(bsl(dst_off + (r0 + j) * 128, 128),
                                                       ps[j][:, 0:64].bitcast(BF16)))

                sec("vector", f_ve)

        def tap_store(name, src_ap, eng="gpsimd"):
            if name not in tap:
                return

            def f(e, src_ap=src_ap, name=name):
                e.wait_ge(S["ve"].h, S["ve"].v)
                e.wait_ge(S["se"].h, S["se"].v)
                STP.issue(e, 0, tap[name], src_ap)

            sec(eng, f)

        nc.const_aps.register(F32, EPS, cst[:, 0:1])
        nc.const_aps.register(F32, -0.5, cst[:, 1:2])

        def f_gp(gpsimd):
            S["ge"].inc(gpsimd.memset(cst[:, 0:1], EPS))
            S["ge"].inc(gpsimd.memset(cst[:, 1:2], -0.5))
            S["ge"].inc(gpsimd.memset(ident16[:], 0.0))
            S["ge"].inc(gpsimd.memset(states[:], 0.0))
            S["ge"].inc(gpsimd.memset(stat[:], 0.0))
            S["ge"].inc(gpsimd.memset(ccol[:], 0.0))
            S["ge"].inc(gpsimd.memset(sc8[:], 0.0))
            S["ge"].inc(gpsimd.memset(jnk[:], 0.0))
            gpsimd.wait_ge(S["ge"].h, S["ge"].v)
            S["ge"].inc(gpsimd.affine_select(out=ident16[:], in_=ident16[:],
                                             compare_op=AL.not_equal, fill=1.0, base=0,
                                             pattern=[[-1, 128]], channel_multiplier=1))

        sec("gpsimd", f_gp)
        ge0 = S["ge"].v

        def f_ld(sync):
            sync.wait_ge(S["ge"].h, ge0)
            for nm, base, cnt in [("nw", 0, 16), ("nb", 16, 16), ("dprm", 32, 16),
                                  ("dpbdh", 48, 8), ("nwdh", 56, 8), ("nbdh", 64, 8)]:
                LV.issue(sync, 0, ccol[:, base:base + cnt],
                         I[nm].rearrange("(a p) -> p a", p=128),
                         allow_slow_non_contiguous=True)
            for dt in range(DHT):
                LV.issue(sync, 0, a_all[:, dt * N:(dt + 1) * N],
                         I["alogdh"][dt * 128:(dt + 1) * 128, :])
            LV.issue(sync, 0, msel_t[:, :], I["msel"].partition_broadcast(128))

        sec("sync", f_ld)

        def f_se(scalar):
            LV.wait_all(scalar)
            S["se"].inc(scalar.activation(a_all[:], a_all[:], AF.Exp))

        sec("scalar", f_se)
        se_a = S["se"].v

        def f_ve(vector):
            vector.wait_ge(S["se"].h, se_a)
            S["ve"].inc(vector.tensor_scalar_mul(a_all[:], a_all[:], -1.0))

        sec("vector", f_ve)

        for ft in range(2):
            vgate = S["ve"].v

            def f(sync, ft=ft, vgate=vgate):
                sync.wait_ge(S["ve"].h, vgate)
                LX.issue(sync, ft, fsl(FA_X + (ft % 2) * 2048, 2048),
                         I["xpw"][ft * 128:(ft + 1) * 128, :])

            sec("sync", f)
            tok = (ft % 2, LX.sems[ft % 2].v)

            def f(vector, tok=tok, ft=ft):
                LX.wait(vector, tok)
                S["ve"].inc(vector.tensor_copy(bsl(BA_PRE, 2048), fsl(FA_X + (ft % 2) * 2048, 2048)))

            sec("vector", f)
            transpose16(BA_PRE, BA_DA)
            ve1 = S["ve"].v

            def f(vector, ft=ft, ve1=ve1):
                vector.wait_ge(S["ve"].h, ve1)
                for K in range(16):
                    S["ve"].inc(vector.tensor_copy(
                        xpwT[:, (K * 2 + ft) * 128:(K * 2 + ft) * 128 + 128],
                        bsl(BA_DA + K * 128, 128)))

            sec("vector", f)

        vgate = S["ve"].v

        def f(sync, vgate=vgate):
            sync.wait_ge(S["ve"].h, vgate)
            LX.issue(sync, 0, fsl(FA_X, 1024).rearrange("p (a r) -> p a r", r=128),
                     I["dpwdh"].rearrange("(a p) r -> p a r", p=128))

        sec("sync", f)
        tok = (0, LX.sems[0].v)

        def f(vector, tok=tok):
            LX.wait(vector, tok)
            S["ve"].inc(vector.tensor_copy(bsl(BA_PRE, 1024), fsl(FA_X, 1024)))

        sec("vector", f)
        transpose16(BA_PRE, BA_DA, nblk=8)
        ve1 = S["ve"].v

        def f(vector, ve1=ve1):
            vector.wait_ge(S["ve"].h, ve1)
            S["ve"].inc(vector.tensor_copy(dpwT[:, :], bsl(BA_DA, 1024)))

        sec("vector", f)
        fence()

        for ot in range(16):
            vgate = S["ve"].v
            sgate = (0, SQW.sems[0].v)

            def f(sync, ot=ot, vgate=vgate):
                sync.wait_ge(S["ve"].h, vgate)
                LX.issue(sync, ot, fsl(FA_X + (ot % 2) * 2048, 2048),
                         I["wout"][ot * 128:(ot + 1) * 128, :])

            sec("sync", f)
            tok = (ot % 2, LX.sems[ot % 2].v)

            def f(vector, tok=tok, ot=ot, sgate=sgate):
                LX.wait(vector, tok)
                SQW.wait(vector, sgate)
                S["ve"].inc(vector.tensor_copy(bsl(BA_PRE, 2048), fsl(FA_X + (ot % 2) * 2048, 2048)))

            sec("vector", f)
            transpose16(BA_PRE, BA_DA)
            ve1 = S["ve"].v

            def f(sync, ot=ot, ve1=ve1):
                sync.wait_ge(S["ve"].h, ve1)
                SQW.issue(sync, 0, d_woT[ot, :, :, :].rearrange("a p c -> p a c"),
                          ab[:, BA_DA:BA_DA + 2048].rearrange("p (a c) -> p a c", c=128))

            sec("sync", f)
        fence()
        tap_store("ccol", ccol[:, :])

        W_STREAMS = [LW, LW2, LW3]
        W_ENGS = ["sync", "scalar", "gpsimd"]
        W_BUFS = [FA_W, FA_X, FA_DL]

        def wbuf(it):
            q = it % 3
            slot = (it // 3) % 2
            return fsl(W_BUFS[q] + slot * 2048, 2048)

        def wload(it, wname, dt, ch, gate):
            q = it % 3

            def f(eng, wname=wname, dt=dt, ch=ch, gate=gate, it=it, q=q):
                if gate is not None:
                    eng.wait_ge(S["se"].h, gate)
                W_STREAMS[q].issue(eng, it // 3, wbuf(it),
                                   I[wname][dt * 128:(dt + 1) * 128, ch * 2048:(ch + 1) * 2048])

            sec(W_ENGS[q], f)
            slot = (it // 3) % 2
            return (slot, W_STREAMS[q].sems[slot].v)

        se_gate = []
        for wi, wname in enumerate(["cfc", "cpr"]):
            for dt in range(DT):
                for ch in range(4):
                    it = wi * 64 + dt * 4 + ch
                    gate = se_gate[it - 6] if it >= 6 else None
                    tok = wload(it, wname, dt, ch, gate)

                    def f(scalar, it=it, tok=tok):
                        W_STREAMS[it % 3].wait(scalar, tok)
                        ck(scalar, "se", scalar.activation(fsl(FA_SQ, 2048),
                                                           wbuf(it),
                                                           AF.Abs,
                                                           accum_out=af[:, FA_XN + it:FA_XN + it + 1]))

                    sec("scalar", f)
                    se_gate.append(S["se"].v)

        def f(vector):
            vector.wait_ge(S["se"].h, S["se"].v)
            ck(vector, "ve", vector.tensor_reduce(stat[:, 8:9],
                                                  af[:, FA_XN:FA_XN + 64], AX.X, AL.add))
            S["ve"].inc(vector.tensor_reduce(stat[:, 9:10],
                                             af[:, FA_XN + 64:FA_XN + 128], AX.X, AL.add))

        sec("vector", f)
        ve_t = S["ve"].v

        def f(sync, ve_t=ve_t):
            sync.wait_ge(S["ve"].h, ve_t)
            SSC.issue(sync, 0, d_sq[0:128, 0:4].bitcast(F32), stat[:, 8:10])
            SSC.wait_all(sync)
            LV.issue(sync, 0, fsl(FA_MISC, 256)[0:1, 0:128],
                     d_sq[0:128, 0:2].bitcast(F32).rearrange("p o -> o p"),
                     allow_slow_non_contiguous=True)
            LV.issue(sync, 0, fsl(FA_MISC, 256)[0:1, 128:256],
                     d_sq[0:128, 2:4].bitcast(F32).rearrange("p o -> o p"),
                     allow_slow_non_contiguous=True)

        sec("sync", f)

        def f(vector):
            LV.wait_all(vector)
            vector.tensor_reduce(fsl(FA_MISC, 512)[0:1, 256:257], fsl(FA_MISC, 512)[0:1, 0:128],
                                 AX.X, AL.add)
            S["ve"].inc(vector.tensor_reduce(fsl(FA_MISC, 512)[0:1, 257:258],
                                             fsl(FA_MISC, 512)[0:1, 128:256], AX.X, AL.add))

        sec("vector", f)
        ve_t2 = S["ve"].v

        def f(sync, ve_t2=ve_t2):
            sync.wait_ge(S["ve"].h, ve_t2)
            SSC.issue(sync, 0, d_scal[0:1, 0:2], fsl(FA_MISC, 512)[0:1, 256:258])
            SSC.wait_all(sync)
            LV.issue(sync, 0, sc8[:, 0:2], d_scal[0:1, 0:2].partition_broadcast(128))

        sec("sync", f)

        def f(vector):
            LV.wait_all(vector)
            ck(vector, "ve", vector.tensor_scalar(sc8[:, 0:2], sc8[:, 0:2],
                                                  1.0 / (D * H), EPS, AL.mult, AL.max))
            vector.reciprocal(sc8[:, 2:4], sc8[:, 0:2])
            vector.tensor_mul(sc8[:, 4:5], sc8[:, 0:1], sc8[:, 0:1])
            S["ve"].inc(vector.tensor_scalar_mul(sc8[:, 5:6], sc8[:, 1:2], 1.0 / 255.0))

        sec("vector", f)
        fence()
        tap_store("sc8", sc8[:, :])

        def sqb(it):
            return fsl(FA_SQ if it % 2 == 0 else FA_XNR, 2048)

        se_gate = []
        ve_sq = []
        for wi, wname in enumerate(["cfc", "cpr"]):
            for dt in range(DT):
                for ch in range(4):
                    it = wi * 64 + dt * 4 + ch
                    gate = se_gate[it - 6] if it >= 6 else None
                    st_tok = (0, SQW.sems[0].v)
                    tok = wload(it, wname, dt, ch, gate)
                    v_prev = ve_sq[it - 2] if it >= 2 else S["ve"].v

                    def f(scalar, wi=wi, it=it, tok=tok, v_prev=v_prev):
                        W_STREAMS[it % 3].wait(scalar, tok)
                        scalar.wait_ge(S["ve"].h, v_prev)
                        S["se"].inc(scalar.activation(sqb(it),
                                                      wbuf(it),
                                                      AF.Copy, scale=sc8[:, 2 + wi:3 + wi]))

                    sec("scalar", f)
                    s0 = S["se"].v
                    se_gate.append(s0)

                    def f(vector, it=it, s0=s0, st_tok=st_tok):
                        vector.wait_ge(S["se"].h, s0)
                        ck(vector, "ve", vector.tensor_scalar(sqb(it), sqb(it),
                                                              1.0, -1.0, AL.min, AL.max))
                        SQW.wait(vector, st_tok)
                        S["ve"].inc(vector.tensor_scalar(bsl(BA_PRE, 2048), sqb(it),
                                                         MAGIC, MAGIC, AL.add, AL.subtract))

                    sec("vector", f)
                    ve_sq.append(S["ve"].v)
                    ve1 = S["ve"].v
                    if wi == 0:
                        def f(sync, dt=dt, ch=ch, ve1=ve1):
                            sync.wait_ge(S["ve"].h, ve1)
                            SQW.issue(sync, 0,
                                      d_q1[dt * 128:(dt + 1) * 128, ch * 2048:(ch + 1) * 2048],
                                      bsl(BA_PRE, 2048))

                        sec("sync", f)
                    else:
                        transpose16(BA_PRE, BA_DA)
                        ve2 = S["ve"].v

                        def f(sync, dt=dt, ch=ch, ve2=ve2):
                            sync.wait_ge(S["ve"].h, ve2)
                            SQW.issue(sync, 0,
                                      d_q2T[ch * 2048:(ch + 1) * 2048, dt * 128:(dt + 1) * 128]
                                      .rearrange("(a p) c -> p a c", p=128),
                                      ab[:, BA_DA:BA_DA + 2048].rearrange("p (a c) -> p a c", c=128))

                        sec("sync", f)
        fence()

        def ln_stats(xoff, w, so, invw):

            s_entry = S["se"].v

            def f(scalar, xoff=xoff, w=w, s_entry=s_entry):
                scalar.wait_ge(S["se"].h, s_entry)
                ck(scalar, "se", scalar.activation(fsl(FA_SQ, w), fsl(xoff, w), AF.Copy,
                                                   accum_out=stat[:, 0:1]))
                S["se"].inc(scalar.activation(fsl(FA_SQ, w), fsl(xoff, w), AF.Square,
                                              accum_out=stat[:, 1:2]))

            sec("scalar", f)
            s1 = S["se"].v

            def f(vector, s1=s1, invw=invw):
                vector.wait_ge(S["se"].h, s1)
                ck(vector, "ve", vector.tensor_scalar_mul(stat[:, 2:3], stat[:, 0:1], invw))
                vector.tensor_mul(stat[:, 3:4], stat[:, 2:3], stat[:, 2:3])
                ck(vector, "ve", vector.tensor_scalar_mul(stat[:, 4:5], stat[:, 1:2], invw))
                S["ve"].inc(vector.tensor_sub(stat[:, 5:6], stat[:, 4:5], stat[:, 3:4]))

            sec("vector", f)
            v1 = S["ve"].v

            def f(scalar, so=so, v1=v1):
                scalar.wait_ge(S["ve"].h, v1)
                ck(scalar, "se", scalar.activation(stat[:, 6:7], stat[:, 5:6], AF.Ln, bias=EPS))
                ck(scalar, "se", scalar.activation(stat[:, so:so + 1], stat[:, 6:7],
                                                   AF.Exp, scale=-0.5))
                ck(scalar, "se", scalar.activation(stat[:, 7:8], stat[:, 2:3], AF.Copy,
                                                   scale=stat[:, so:so + 1]))
                ck(scalar, "se", scalar.activation(stat[:, so + 1:so + 2], stat[:, 7:8],
                                                   AF.Copy, scale=-1.0))
                S["se"].inc(scalar.nop())

            sec("scalar", f)

        def ph_pre(sc):
            se_xfree = []
            for tb in range(16):
                gate = se_xfree[tb - 2] if tb >= 2 else None

                def f(sync, sc=sc, tb=tb, gate=gate):
                    if gate is not None:
                        sync.wait_ge(S["se"].h, gate)
                    LX.issue(sync, tb, fsl(FA_X + (tb % 2) * 2048, 2048),
                             I["xsc"][sc * TSC + tb * 128: sc * TSC + (tb + 1) * 128, :])

                sec("sync", f)
                tok = (tb % 2, LX.sems[tb % 2].v)

                def f(scalar, tok=tok):
                    LX.wait(scalar, tok)

                sec("scalar", f)
                ln_stats(FA_X + (tb % 2) * 2048, 2048, 8, INVD)

                def f(scalar, tb=tb):
                    S["se"].inc(scalar.activation(fsl(FA_XN, 2048), fsl(FA_X + (tb % 2) * 2048, 2048),
                                                  AF.Identity, scale=stat[:, 8:9], bias=stat[:, 9:10]))

                sec("scalar", f)
                se_xfree.append(S["se"].v)
                ln_stats(FA_XN, 2048, 10, INVD)

                def f(scalar, tb=tb):
                    S["se"].inc(scalar.activation(fsl(FA_XNR, 2048), fsl(FA_XN, 2048),
                                                  AF.Identity, scale=stat[:, 10:11], bias=stat[:, 11:12]))

                sec("scalar", f)
                s2 = S["se"].v

                def f(vector, tb=tb, s2=s2):
                    vector.wait_ge(S["se"].h, s2)
                    vector.tensor_mul(stat[:, 16 + tb:17 + tb], stat[:, 8:9], stat[:, 10:11])
                    ck(vector, "ve", vector.tensor_mul(stat[:, 12:13], stat[:, 9:10], stat[:, 10:11]))
                    vector.tensor_add(stat[:, 32 + tb:33 + tb], stat[:, 12:13], stat[:, 11:12])
                    S["ve"].inc(vector.tensor_copy(bsl(BA_PRE, 2048), fsl(FA_XNR, 2048)))

                sec("vector", f)
                transpose16(BA_PRE, BA_DA)
                v2 = S["ve"].v

                def f(vector, v2=v2):
                    vector.wait_ge(S["ve"].h, v2)
                    for dt in range(DT):
                        S["ve"].inc(vector.tensor_scalar(
                            bsl(BA_DA + dt * 128, 128), bsl(BA_DA + dt * 128, 128),
                            ccol[:, dt:dt + 1], ccol[:, 16 + dt:17 + dt], AL.mult, AL.add))

                sec("vector", f)
                v3 = S["ve"].v

                def f(tensor, tb=tb, v3=v3):
                    tensor.wait_ge(S["ve"].h, v3)
                    for ft in range(2):
                        for K in range(16):
                            st_ap = xpwT[:, (K * 2 + ft) * 128:(K * 2 + ft) * 128 + 128]
                            mm = tensor.matmul(ps[4 + ft][:, 0:128], st_ap,
                                               bsl(BA_DA + K * 128, 128), start=(K == 0), stop=(K == 15))
                    S["pe"].inc(mm)

                sec("tensor", f)
                pe1 = S["pe"].v

                def f(vector, tb=tb, pe1=pe1):
                    vector.wait_ge(S["pe"].h, pe1)
                    vector.tensor_copy(ab[:, BA_DTR + tb * 128:BA_DTR + (tb + 1) * 128], ps[4][:, 0:128])
                    S["ve"].inc(vector.tensor_copy(ab[:, BA_BCROW + tb * 128:BA_BCROW + (tb + 1) * 128],
                                                   ps[5][:, 0:128]))

                sec("vector", f)
            vbc = S["ve"].v

            def f(sync, sc=sc, vbc=vbc):
                sync.wait_ge(S["ve"].h, vbc)
                SB.issue(sync, 0, d_bc[:, :], bsl(BA_BCROW, 2048))
                if sc == 0:
                    SB.wait_all(sync)
                    for rep in range(4):
                        SB.issue(sync, 0, d_c0[:, rep * 512:(rep + 1) * 512],
                                 ab[64:128, BA_BCROW:BA_BCROW + 512])
                        SB.wait_all(sync)

            sec("sync", f)

            se_dhfree = []
            ve_xnread = []
            v_affine = S["ve"].v
            for tb in range(16):
                gate = se_dhfree[tb - 2] if tb >= 2 else None

                def f(sync, sc=sc, tb=tb, gate=gate):
                    if gate is not None:
                        sync.wait_ge(S["se"].h, gate)
                    LX.issue(sync, tb, fsl(FA_X + (tb % 2) * 1024, 1024),
                             I["xdh"][sc * TSC + tb * 128: sc * TSC + (tb + 1) * 128, :])

                sec("sync", f)
                tok = (tb % 2, LX.sems[tb % 2].v)

                vgate = ve_xnread[tb - 1] if tb >= 1 else v_affine

                def f(scalar, tb=tb, tok=tok, vgate=vgate):
                    LX.wait(scalar, tok)
                    scalar.wait_ge(S["ve"].h, vgate)
                    S["se"].inc(scalar.activation(fsl(FA_XN, 1024), fsl(FA_X + (tb % 2) * 1024, 1024),
                                                  AF.Identity, scale=stat[:, 16 + tb:17 + tb],
                                                  bias=stat[:, 32 + tb:33 + tb]))

                sec("scalar", f)
                se_dhfree.append(S["se"].v)
                s0 = S["se"].v

                def f(vector, s0=s0):
                    vector.wait_ge(S["se"].h, s0)
                    S["ve"].inc(vector.tensor_copy(bsl(BA_PRE, 1024), fsl(FA_XN, 1024)))

                sec("vector", f)
                ve_xnread.append(S["ve"].v)
                transpose16(BA_PRE, BA_DA, nblk=8)
                v1 = S["ve"].v

                def f(vector, tb=tb, v1=v1):
                    vector.wait_ge(S["ve"].h, v1)
                    for dt in range(DHT):
                        S["ve"].inc(vector.tensor_scalar(
                            ab[:, BA_XN2 + dt * 2048 + tb * 128: BA_XN2 + dt * 2048 + (tb + 1) * 128],
                            bsl(BA_DA + dt * 128, 128),
                            ccol[:, 56 + dt:57 + dt], ccol[:, 64 + dt:65 + dt], AL.mult, AL.add))

                sec("vector", f)
            fence()
            if "xn2dh" in tap:
                def f(gpsimd, sc=sc):
                    STP.issue(gpsimd, 0,
                              tap["xn2dh"][:, sc * TSC:(sc + 1) * TSC].rearrange("(a p) c -> p a c", p=128),
                              ab[:, BA_XN2:BA_XN2 + 16384].rearrange("p (a c) -> p a c", c=2048))

                sec("gpsimd", f)
            if "dtraw" in tap:
                def f(gpsimd, sc=sc):
                    STP.issue(gpsimd, 0, tap["dtraw"][:, sc * TSC:(sc + 1) * TSC], bsl(BA_DTR, 2048))

                sec("gpsimd", f)
            if "bc" in tap:
                def f(gpsimd, sc=sc):
                    STP.issue(gpsimd, 0, tap["bc"][:, sc * TSC:(sc + 1) * TSC], bsl(BA_BCROW, 2048))

                sec("gpsimd", f)
            if "stats" in tap and sc == 0:
                def f(gpsimd):
                    gpsimd.wait_ge(S["ve"].h, S["ve"].v)
                    STP.issue(gpsimd, 0, tap["stats"], stat[:, :])

                sec("gpsimd", f)
            fence()

        def ph_scan(sc):
            NI = DHT * N
            b_exp = S["exp"].v
            b_h = S["h"].v
            b_g = S["g"].v
            b_y = S["y"].v
            b_yev = S["yev"].v
            b_dl = S["dl"].v
            b_dp = S["dp"].v
            b_dcp = S["dcp"].v
            b_ab = S["ab2"].v
            b_ln = S["ln"].v
            b_bc = [BCS.sems[0].v, BCS.sems[1].v]
            b_sy = SY.sems[0].v
            b_pe = S["pe"].v
            b_lq = [LQ.sems[0].v, LQ.sems[1].v]
            b_sqs = [s.v for s in SQS.sems]
            b_fh = S["fh"].v
            b_fe = S["fe"].v

            do_h1 = "ffn" in phases

            def fe_stt(it):
                return b_fe + 3 * it + 1 + (2 if it >= 128 else 0)

            def bank(dt, k):
                return ps[(dt % 2) * 4 + k]

            def hbank(c, j):
                return ps[(c % 2) * 4 + (j % 2)]

            def h1_copy(scalar, it):
                c = it // 32
                j = it % 4
                scalar.wait_ge(S["pe"].h, b_pe + it + 1)
                if it >= 32:
                    scalar.wait_ge(S["fe"].h, fe_stt(it - 32))
                scalar.activation(fh1v(it % 32), hbank(c, j)[:, 0:256],
                                  AF.Copy).then_inc(S["fh"].h, 1)
                if it % 4 == 3 and it // 4 + 2 < 64:
                    lq_issue(scalar, it // 4 + 2)

            def evac_chunk(vector, c):
                for k32 in range(32):
                    it = c * 32 + k32
                    j = it % 4
                    vector.wait_ge(S["fh"].h, b_fh + it + 1)
                    if it >= 8:
                        q8 = (it - 8) % 8
                        vector.wait_ge(SQS.sems[q8].h, b_sqs[q8] + 16 * ((it - 8) // 8 + 1))
                    ck(vector, "fe", vector.scalar_tensor_tensor(
                        fsqv(it % 8), fh1v(it % 32), 0.0, fh1v(it % 32), AL.max, AL.mult))
                    ck(vector, "fe", vector.tensor_reduce(
                        stat[:, 12:13], fsqv(it % 8), AX.X, AL.max))
                    ck(vector, "fe", vector.tensor_tensor(stat[:, 8 + j:9 + j],
                                                          stat[:, 8 + j:9 + j],
                                                          stat[:, 12:13], AL.max))
                if c == 3:
                    ck(vector, "fe", vector.tensor_copy(stat[:, 16:20], stat[:, 8:12]))
                    ck(vector, "fe", vector.memset(stat[:, 8:12], 0.0))
                    fe_mark[(sc, 0)] = S["fe"].v
                if c == DHT - 1:
                    fe_mark[(sc, 1)] = S["fe"].v

            def f(vector):
                vector.wait_ge(S["se"].h, S["se"].v)
                S["ve"].inc(vector.memset(stat[:, 8:12], 0.0))

            sec("vector", f)
            v_init = S["ve"].v

            def lq_issue(eng, hc):
                LQ.issue(eng, hc, q1v(hc % 2).rearrange("p (a c) -> p a c", c=256),
                         d_q1[:, (hc % 32) * 256:(hc % 32 + 1) * 256]
                         .rearrange("(a p) c -> p a c", p=128))

            def f_sync(sync, sc=sc):
                SB.wait_all(sync)
                for dt in range(DHT):
                    for n in range(N):
                        i = dt * N + n
                        p = i % 2
                        if i >= 2:
                            sync.wait_ge(S["g"].h, b_g + i - 1)
                        BCS.issue(sync, p, bsl(BA_BCB + p * 2048, 2048),
                                  d_bc[n:n + 1, :].partition_broadcast(128))
                        BCS.sems[p].inc(sync.dma_start(out=bsl(BA_BCC + p * 2048, 2048),
                                                       in_=d_c0[n:n + 1, :].partition_broadcast(128)), 16)

            sec("sync", f_sync)

            def h1_group(tensor, it):
                c = it // 32
                hc = it // 4
                quad = hc // 32
                j = it % 4
                if it % 32 == 0:
                    tensor.wait_ge(S["yev"].h, b_yev + c + 1)
                if j == 0:
                    tensor.wait_ge(LQ.sems[hc % 2].h, b_lq[hc % 2] + 16 * (hc // 2 + 1))
                if it >= 2:
                    tensor.wait_ge(S["fh"].h, b_fh + it - 1)
                else:
                    tensor.wait_ge(S["ve"].h, v_init)
                for K in range(16):
                    mm = tensor.matmul(hbank(c, j)[:, 0:256],
                                       xnqv(quad)[:, K * 512 + j * 128:K * 512 + j * 128 + 128],
                                       q1v(hc % 2)[:, K * 256:(K + 1) * 256],
                                       start=(K == 0), stop=(K == 15))
                mm.then_inc(S["pe"].h, 1)

            def f_pe(tensor, sc=sc):
                for dt in range(DHT):
                    if dt >= 2:
                        tensor.wait_ge(S["yev"].h, b_yev + dt - 1)
                        if do_h1:
                            tensor.wait_ge(S["fh"].h, b_fh + 32 * (dt - 1))
                    for k in range(4):
                        mm = tensor.matmul(bank(dt, k)[:, :], dpwT[:, dt * 128:(dt + 1) * 128],
                                           bsl(BA_DTR + k * 512, 512), start=True, stop=True)
                    mm.then_inc(S["dp"].h, 1)
                    for n in range(N):
                        i = dt * N + n
                        tensor.wait_ge(S["g"].h, b_g + i + 1)
                        for k in range(4):
                            mm = tensor.matmul(bank(dt, k)[:, :], ident16[:],
                                               bsl(BA_HH + (i % 2) * 2048 + k * 512, 512),
                                               start=(n == 0), stop=(n == N - 1))
                        mm.then_inc(S["y"].h, 1)
                        if do_h1 and dt >= 1 and n % 2 == 1:
                            h1_group(tensor, 32 * (dt - 1) + n // 2)
                for it in (range(32 * (DHT - 1), 32 * DHT) if do_h1 else ()):
                    h1_group(tensor, it)
                S["dp"].v += DHT
                S["y"].v += NI
                if do_h1:
                    S["pe"].v += 256

            sec("tensor", f_pe)

            def f_se(scalar, sc=sc):
                if do_h1:
                    lq_issue(scalar, 0)
                    lq_issue(scalar, 1)
                for dt in range(DHT):
                    scalar.wait_ge(S["dp"].h, b_dp + dt + 1)
                    dloff = FA_DL + (dt % 2) * 2048
                    for k in range(4):
                        ac = scalar.activation(fsl(dloff + k * 512, 512), bank(dt, k)[:, :],
                                               AF.Identity, bias=ccol[:, 48 + dt:49 + dt])
                    ac.then_inc(S["dcp"].h, 1)
                    scalar.wait_ge(S["ab2"].h, b_ab + dt + 1)
                    ck(scalar, "se", scalar.activation(fsl(FA_SP, 2048), fsl(FA_SP, 2048),
                                                       AF.Exp, scale=-1.0))
                    scalar.activation(fsl(FA_SP, 2048), fsl(FA_SP, 2048), AF.Ln, bias=1.0
                                      ).then_inc(S["ln"].h, 1)
                    scalar.wait_ge(S["dl"].h, b_dl + dt + 1)
                    for n in range(N):
                        i = dt * N + n
                        if i >= 2:
                            scalar.wait_ge(S["h"].h, b_h + i - 1)
                        scalar.activation(bsl(BA_DA + (i % 2) * 2048, 2048), fsl(dloff, 2048),
                                          AF.Exp, scale=a_all[:, dt * N + n:dt * N + n + 1]
                                          ).then_inc(S["exp"].h, 1)
                        if do_h1 and dt >= 1 and n % 2 == 0 and n >= 2:
                            h1_copy(scalar, 32 * (dt - 1) + n // 2 - 1)
                    if do_h1 and dt >= 1:
                        h1_copy(scalar, 32 * (dt - 1) + 31)
                    scalar.wait_ge(S["y"].h, b_y + (dt + 1) * N)
                    scalar.wait_ge(SY.sems[0].h, b_sy + dt * 16)
                    for k in range(4):
                        ac = scalar.activation(bsl(BA_Y16 + k * 512, 512), bank(dt, k)[:, :], AF.Copy)
                    ac.then_inc(S["yev"].h, 1)
                for it in (range(32 * (DHT - 1), 32 * DHT) if do_h1 else ()):
                    h1_copy(scalar, it)
                S["dcp"].v += DHT
                S["ln"].v += DHT
                S["exp"].v += NI
                S["yev"].v += DHT
                if do_h1:
                    S["fh"].v += 256

            sec("scalar", f_se)

            def f_ve(vector, sc=sc):
                for dt in range(DHT):
                    dloff = FA_DL + (dt % 2) * 2048
                    vector.wait_ge(S["dcp"].h, b_dcp + dt + 1)
                    vector.scalar_tensor_tensor(fsl(FA_SP, 2048), fsl(dloff, 2048), -1.0,
                                                fsl(dloff, 2048), AL.mult, AL.max
                                                ).then_inc(S["ab2"].h, 1)
                    vector.wait_ge(S["ln"].h, b_ln + dt + 1)
                    ck(vector, "ve", vector.scalar_tensor_tensor(fsl(dloff, 2048), fsl(dloff, 2048),
                                                                 0.0, fsl(FA_SP, 2048),
                                                                 AL.max, AL.add))
                    vector.tensor_tensor(bsl(BA_S16, 2048), fsl(dloff, 2048),
                                         bsl(BA_XN2 + dt * 2048, 2048), AL.mult
                                         ).then_inc(S["dl"].h, 1)
                    for n in range(N):
                        i = dt * N + n
                        p = i % 2
                        vector.wait_ge(BCS.sems[p].h, b_bc[p] + 32 * (i // 2 + 1))
                        vector.wait_ge(S["exp"].h, b_exp + i + 1)
                        if i >= 2:
                            vector.wait_ge(S["y"].h, b_y + i - 1)
                        ck(vector, "ve", vector.tensor_tensor(bsl(BA_U + p * 2048, 2048),
                                                               bsl(BA_S16, 2048),
                                                               bsl(BA_BCB + p * 2048, 2048), AL.mult))
                        ck(vector, "ve", vector.tensor_tensor_scan(bsl(BA_HH + p * 2048, 2048),
                                                                   bsl(BA_DA + p * 2048, 2048),
                                                                   bsl(BA_U + p * 2048, 2048),
                                                                   states[:, dt * N + n:dt * N + n + 1],
                                                                   AL.mult, AL.add))
                        vector.tensor_copy(states[:, dt * N + n:dt * N + n + 1],
                                           bsl(BA_HH + p * 2048, 2048)[:, 2047:2048]
                                           ).then_inc(S["h"].h, 1)
                        vector.wait_ge(S["h"].h, b_h + i + 1)
                        vector.tensor_tensor(bsl(BA_HH + p * 2048, 2048), bsl(BA_HH + p * 2048, 2048),
                                             bsl(BA_BCC + p * 2048, 2048), AL.mult
                                             ).then_inc(S["g"].h, 1)
                    if do_h1 and dt >= 1:
                        evac_chunk(vector, dt - 1)
                if do_h1:
                    evac_chunk(vector, DHT - 1)
                S["ab2"].v += DHT
                S["dl"].v += DHT
                S["h"].v += NI
                S["g"].v += NI

            sec("vector", f_ve)

            def sqs_chunk(gpsimd, c):
                for k32 in range(32):
                    it = c * 32 + k32
                    quad = it // 128
                    j = it % 4
                    hcq = (it // 4) % 32
                    gpsimd.wait_ge(S["fe"].h, fe_stt(it))
                    SQS.issue(gpsimd, it,
                              d_sq[quad * 512 + j * 128:quad * 512 + (j + 1) * 128,
                                   hcq * 256:(hcq + 1) * 256],
                              fsqv(it % 8))

            def f_gp(gpsimd, sc=sc):
                for dt in range(DHT):
                    gpsimd.wait_ge(S["yev"].h, b_yev + dt + 1)
                    SY.issue(gpsimd, 0, d_yin[sc].ap()[dt * 128:(dt + 1) * 128, :],
                             bsl(BA_Y16, 2048))
                    if do_h1 and dt >= 1:
                        sqs_chunk(gpsimd, dt - 1)
                if do_h1:
                    sqs_chunk(gpsimd, DHT - 1)
                SY.wait_all(gpsimd)
                gpsimd.collective_compute(
                    "AllGather", AL.bypass,
                    replica_groups=[[0, 1], [2, 3], [4, 5], [6, 7]],
                    ins=[d_yin[sc].ap().opt()],
                    outs=[d_yout[sc].ap().opt()],
                ).then_inc(S["ag"].h, 1)
                S["ag"].v += 1

            sec("gpsimd", f_gp)
            fence()
            if "y" in tap:
                def f(gpsimd, sc=sc):
                    STP.issue(gpsimd, 0,
                              tap["y"][:, sc * TSC:(sc + 1) * TSC].rearrange("(a p) c -> a p c", p=128),
                              d_yin[sc].ap().rearrange("(a p) c -> a p c", p=128))

                sec("gpsimd", f)
            fence()

        def xnqv(quad):
            base = FA_W if quad == 0 else 16384
            return af[:, base:base + 4096].bitcast(BF16)

        def q1v(p):
            return af[:, p * 2048:(p + 1) * 2048].bitcast(BF16)

        def fh1v(s):
            return af[:, 4096 + s * 128:4096 + (s + 1) * 128].bitcast(BF16)

        def fsqv(s):
            return af[:, 8192 + s * 128:8192 + (s + 1) * 128].bitcast(BF16)

        fe_mark = {}

        def ph_ffn_stats(sc):
            se_xfree = []
            for tb in range(8):
                quad, j = tb // 4, tb % 4
                gate = se_xfree[tb - 2] if tb >= 2 else None

                def f(sync, sc=sc, tb=tb, gate=gate):
                    if gate is not None:
                        sync.wait_ge(S["se"].h, gate)
                    LX.issue(sync, tb, fsl(FA_X + (tb % 2) * 2048, 2048),
                             I["xq"][sc * QT + tb * 128: sc * QT + (tb + 1) * 128, :])

                sec("sync", f)
                tok = (tb % 2, LX.sems[tb % 2].v)

                def f(scalar, tok=tok):
                    LX.wait(scalar, tok)

                sec("scalar", f)
                ln_stats(FA_X + (tb % 2) * 2048, 2048, 8, INVD)

                def f(scalar, tb=tb):
                    scalar.activation(stat[:, 48 + tb:49 + tb], stat[:, 8:9], AF.Copy)
                    scalar.activation(stat[:, 56 + tb:57 + tb], stat[:, 9:10], AF.Copy)
                    S["se"].inc(scalar.activation(fsl(FA_XN, 2048),
                                                  fsl(FA_X + (tb % 2) * 2048, 2048),
                                                  AF.Identity, scale=stat[:, 8:9],
                                                  bias=stat[:, 9:10]))

                sec("scalar", f)
                se_xfree.append(S["se"].v)
                s1 = S["se"].v

                def f(vector, s1=s1):
                    vector.wait_ge(S["se"].h, s1)
                    S["ve"].inc(vector.tensor_copy(bsl(BA_PRE, 2048), fsl(FA_XN, 2048)))

                sec("vector", f)
                transpose16(BA_PRE, BA_DA)
                v0 = S["ve"].v

                def f(vector, quad=quad, j=j, v0=v0):
                    vector.wait_ge(S["ve"].h, v0)
                    for dt in range(DT):
                        S["ve"].inc(vector.tensor_copy(
                            xnqv(quad)[:, dt * 512 + j * 128: dt * 512 + j * 128 + 128],
                            bsl(BA_DA + dt * 128, 128)))

                sec("vector", f)

        def ph_ffn_tail(sc):
            for quad in range(2):
                kb = 16 if quad == 0 else 8
                femk = fe_mark[(sc, quad)]

                def f(vector, sc=sc, quad=quad, kb=kb, femk=femk):
                    vector.wait_ge(S["fe"].h, femk)
                    for j in range(4):
                        ck(vector, "ve", vector.tensor_scalar(stat[:, 12:13], stat[:, kb + j:kb + j + 1],
                                                              sc8[:, 4:5], EPS, AL.mult, AL.max))
                        ck(vector, "ve", vector.reciprocal(stat[:, 13:14], stat[:, 12:13]))
                        vector.tensor_scalar(stat[:, 13:14], stat[:, 13:14],
                                             sc8[:, 4:5], 255.0, AL.mult, AL.mult)
                        ck(vector, "ve", vector.tensor_tensor(stat[:, 14:15], stat[:, 12:13],
                                                              sc8[:, 5:6], AL.mult))
                        vector.tensor_copy(stat[:, 24 + j:25 + j], stat[:, 13:14])
                        S["ve"].inc(vector.tensor_copy(stat[:, 28 + j:29 + j], stat[:, 14:15]))

                sec("vector", f)
                v_k = S["ve"].v

                def f(gpsimd, sc=sc, quad=quad, v_k=v_k):
                    gpsimd.wait_ge(S["ve"].h, v_k)
                    for j in range(4):
                        SPK.issue(gpsimd, 0,
                                  d_pk[sc, (quad * 4 + j) * 128:(quad * 4 + j + 1) * 128]
                                  .rearrange("(p o) -> p o", o=1),
                                  stat[:, 28 + j:29 + j])

                sec("gpsimd", f)

                ve_round = []
                b_lsq = [LSQ.sems[0].v, LSQ.sems[1].v]
                for j in range(4):
                    for ch in range(4):
                        it = j * 4 + ch
                        gate = ve_round[it - 2] if it >= 2 else None

                        def f(sync, sc=sc, quad=quad, j=j, ch=ch, it=it, gate=gate):
                            if it == 0:
                                SQS.wait_all(sync)
                                sync.wait_ge(S["ve"].h, S["ve"].v)
                                sync.wait_ge(S["fe"].h, S["fe"].v)
                            if gate is not None:
                                sync.wait_ge(S["ve"].h, gate)
                            LSQ.issue(sync, it, bsl(BA_SQR + (it % 2) * 2048, 2048),
                                      d_sq[quad * 512 + j * 128:quad * 512 + (j + 1) * 128,
                                           ch * 2048:(ch + 1) * 2048])

                        sec("sync", f)
                        tok = (it % 2, LSQ.sems[it % 2].v)

                        def f(vector, j=j, it=it, tok=tok):
                            LSQ.wait(vector, tok)
                            ck(vector, "ve", vector.tensor_scalar(bsl(BA_SQR + (it % 2) * 2048, 2048),
                                                                  bsl(BA_SQR + (it % 2) * 2048, 2048),
                                                                  stat[:, 24 + j:25 + j], None, AL.mult))
                            S["ve"].inc(vector.tensor_scalar(bsl(BA_SQR + (it % 2) * 2048, 2048),
                                                             bsl(BA_SQR + (it % 2) * 2048, 2048),
                                                             MAGIC, MAGIC, AL.add, AL.subtract))

                        sec("vector", f)
                        for r0 in (0, 8):
                            ve0 = S["ve"].v

                            def f_pe(tensor, it=it, r0=r0, ve0=ve0):
                                tensor.wait_ge(S["ve"].h, ve0)
                                for b in range(8):
                                    tr = tensor.transpose(ps[b][:, 0:64].bitcast(BF16),
                                                          bsl(BA_SQR + (it % 2) * 2048 + (r0 + b) * 128, 128),
                                                          ident16[:])
                                S["pe"].inc(tr)

                            sec("tensor", f_pe)
                            pe0 = S["pe"].v

                            def f_ve(vector, j=j, ch=ch, r0=r0, pe0=pe0):
                                vector.wait_ge(S["pe"].h, pe0)
                                for b in range(8):
                                    K = ch * 16 + r0 + b
                                    S["ve"].inc(vector.tensor_copy(
                                        ab[:, BA_HQT + K * 512 + j * 128: BA_HQT + K * 512 + (j + 1) * 128],
                                        ps[b][:, 0:64].bitcast(BF16)))

                            sec("vector", f_ve)
                        ve_round.append(S["ve"].v)

                b_f2 = S["dcp"].v
                lp0 = LPK.sems[0].v

                def f(sync, sc=sc, quad=quad):
                    SPK.wait_all(sync)
                    LPK.issue(sync, 0, fsl(FA_PK, 512),
                              d_pk[sc, quad * 512:(quad + 1) * 512].partition_broadcast(128))

                sec("sync", f)
                lptok = (0, LPK.sems[0].v)
                vhq = S["ve"].v
                def q2buf(p):
                    return (bsl(0, 8192) if p == 0
                            else af[:, FA_W:FA_W + 4096].bitcast(BF16))

                for dt in range(DT):
                    def f(sync, sc=sc, quad=quad, dt=dt, b_f2=b_f2, vhq=vhq):
                        if dt >= 2:
                            sync.wait_ge(S["dcp"].h, b_f2 + dt - 1)
                        else:
                            sync.wait_ge(S["ve"].h, vhq)
                        LQ2.issue(sync, dt, q2buf(dt % 2)
                                  .rearrange("p (a c) -> p a c", c=128),
                                  d_q2T[:, dt * 128:(dt + 1) * 128]
                                  .rearrange("(a p) c -> p a c", p=128))

                    sec("sync", f)
                    tok = (dt % 2, LQ2.sems[dt % 2].v)

                    def f_pe(tensor, dt=dt, tok=tok, vhq=vhq):
                        LQ2.wait(tensor, tok)
                        tensor.wait_ge(S["ve"].h, vhq)
                        for K in range(64):
                            mm = tensor.matmul(ps[4 + dt % 2][:, 0:512],
                                               q2buf(dt % 2)[:, K * 128:(K + 1) * 128],
                                               bsl(BA_HQT + K * 512, 512),
                                               start=(K == 0), stop=(K == 63))
                        S["pe"].inc(mm)

                    sec("tensor", f_pe)
                    pe0 = S["pe"].v

                    def f(vector, dt=dt, pe0=pe0, lptok=lptok, b_f2=b_f2):
                        vector.wait_ge(S["pe"].h, pe0)
                        LPK.wait(vector, lptok)
                        if dt >= 1:
                            vector.wait_ge(S["dcp"].h, b_f2 + dt)
                        S["ve"].inc(vector.tensor_tensor(fsl(FA_FT, 512), ps[4 + dt % 2][:, 0:512],
                                                         fsl(FA_PK, 512), AL.mult))

                    sec("vector", f)
                    v1 = S["ve"].v
                    sg_tok = (0, SG.sems[0].v)

                    def f(scalar, dt=dt, v1=v1, sg_tok=sg_tok):
                        scalar.wait_ge(S["ve"].h, v1)
                        SG.wait(scalar, sg_tok)
                        scalar.activation(bsl(BA_SILO, 512), fsl(FA_FT, 512), AF.Silu
                                          ).then_inc(S["dcp"].h, 1)

                    sec("scalar", f)
                    S["dcp"].v += 1
                    sd = S["dcp"].v

                    def f(gpsimd, sc=sc, quad=quad, dt=dt, sd=sd):
                        gpsimd.wait_ge(S["dcp"].h, sd)
                        SG.issue(gpsimd, 0, d_g[sc, dt, :, quad * 512:(quad + 1) * 512],
                                 bsl(BA_SILO, 512))

                    sec("gpsimd", f)
            fence()
            if "g" in tap:
                def f(gpsimd, sc=sc):
                    SG.wait_all(gpsimd)
                    STP.issue(gpsimd, 0,
                              tap["g"][:, sc * QT:(sc + 1) * QT].rearrange("(a p) c -> a p c", p=128),
                              d_g[sc, :, :, :])

                sec("gpsimd", f)
            fence()

        def ph_post(sc):
            se_xfree = []
            ve_xnread = []
            v_entry = S["ve"].v
            for tb in range(8):
                gate = se_xfree[tb - 2] if tb >= 2 else None
                vgate = ve_xnread[tb - 1] if tb >= 1 else v_entry

                def f(sync, sc=sc, tb=tb, gate=gate):
                    if gate is not None:
                        sync.wait_ge(S["se"].h, gate)
                    LX.issue(sync, tb, fsl(FA_X + (tb % 2) * 2048, 2048),
                             I["xq"][sc * QT + tb * 128: sc * QT + (tb + 1) * 128, :])

                sec("sync", f)
                tok = (tb % 2, LX.sems[tb % 2].v)

                def f(scalar, tb=tb, tok=tok, vgate=vgate):
                    LX.wait(scalar, tok)
                    scalar.wait_ge(S["ve"].h, vgate)
                    S["se"].inc(scalar.activation(fsl(FA_XN, 2048), fsl(FA_X + (tb % 2) * 2048, 2048),
                                                  AF.Identity, scale=stat[:, 48 + tb:49 + tb],
                                                  bias=stat[:, 56 + tb:57 + tb]))

                sec("scalar", f)
                se_xfree.append(S["se"].v)
                s0 = S["se"].v

                def f(vector, s0=s0):
                    vector.wait_ge(S["se"].h, s0)
                    S["ve"].inc(vector.tensor_copy(bsl(BA_PRE, 2048), fsl(FA_XN, 2048)))

                sec("vector", f)
                ve_xnread.append(S["ve"].v)
                transpose16(BA_PRE, BA_DA)
                v1 = S["ve"].v

                def f(vector, tb=tb, v1=v1):
                    vector.wait_ge(S["ve"].h, v1)
                    for dt in range(DT):
                        S["ve"].inc(vector.tensor_copy(
                            ab[:, BA_V + dt * 1024 + tb * 128: BA_V + dt * 1024 + (tb + 1) * 128],
                            bsl(BA_DA + dt * 128, 128)))

                sec("vector", f)

            agw = S["ag"].v
            v_p1 = S["ve"].v

            def f(sync, agw=agw, v_p1=v_p1):
                sync.wait_ge(S["ag"].h, agw)
                sync.wait_ge(S["ve"].h, v_p1)

            sec("sync", f)
            blend_done = []
            for dt in range(DT):
                gate = blend_done[dt - 2] if dt >= 2 else None

                def f(sync, sc=sc, dt=dt, gate=gate):
                    if gate is not None:
                        sync.wait_ge(S["ve"].h, gate)
                    LY.issue(sync, dt, ab[:, BA_Y01 + (dt % 2) * 2048:BA_Y01 + (dt % 2) * 2048 + 1024],
                             d_yout[sc].ap()[dt * 128:(dt + 1) * 128, 0:QT])
                    LY.sems[dt % 2].inc(sync.dma_start(
                        out=ab[:, BA_Y01 + (dt % 2) * 2048 + 1024:BA_Y01 + (dt % 2) * 2048 + 2048],
                        in_=d_yout[sc].ap()[dt * 128:(dt + 1) * 128, QT:TSC]), 16)

                sec("sync", f)
                tok = (dt % 2, LY.sems[dt % 2].v)

                def f(vector, dt=dt, tok=tok):
                    LY.wait(vector, tok)
                    y0 = ab[:, BA_Y01 + (dt % 2) * 2048:BA_Y01 + (dt % 2) * 2048 + 1024]
                    y1 = ab[:, BA_Y01 + (dt % 2) * 2048 + 1024:BA_Y01 + (dt % 2) * 2048 + 2048]
                    vv = ab[:, BA_V + dt * 1024:BA_V + (dt + 1) * 1024]
                    vector.tensor_scalar(y0, y0, msel_t[:, 0:1], None, AL.mult)
                    ck(vector, "ve", vector.tensor_scalar(y1, y1, msel_t[:, 1:2], None, AL.mult))
                    vector.tensor_add(y0, y0, y1)
                    ck(vector, "ve", vector.tensor_scalar(vv, vv, ccol[:, 32 + dt:33 + dt],
                                                          None, AL.mult))
                    S["ve"].inc(vector.tensor_add(vv, vv, y0))

                sec("vector", f)
                blend_done.append(S["ve"].v)
            v_v = S["ve"].v
            if "vtap" in tap:
                def f(gpsimd, sc=sc, v_v=v_v):
                    gpsimd.wait_ge(S["ve"].h, v_v)
                    STP.issue(gpsimd, 0,
                              tap["vtap"][:, sc * QT:(sc + 1) * QT].rearrange("(a p) c -> p a c", p=128),
                              ab[:, BA_V:BA_V + 16384].rearrange("p (a c) -> p a c", c=1024))

                sec("gpsimd", f)

            fin_done = []
            for ot in range(16):
                wgate = fin_done[ot - 2] if ot >= 2 else None

                def f(sync, sc=sc, ot=ot, v_v=v_v, wgate=wgate):
                    sync.wait_ge(S["ve"].h, v_v if wgate is None else wgate)
                    LWO.issue(sync, ot, bsl(BA_WOS + (ot % 2) * 2048, 2048)
                              .rearrange("p (a c) -> p a c", c=128),
                              d_woT[ot, :, :, :].rearrange("a p c -> p a c"))
                    LG.issue(sync, ot, ab[:, BA_GP + (ot % 2) * 1024:BA_GP + (ot % 2) * 1024 + 1024],
                             d_g[sc, ot, :, :])

                sec("sync", f)
                tokw = (ot % 2, LWO.sems[ot % 2].v)
                tokg = (ot % 2, LG.sems[ot % 2].v)
                vfin = S["ve"].v

                def f_pe(tensor, ot=ot, tokw=tokw, vfin=vfin):
                    LWO.wait(tensor, tokw)
                    tensor.wait_ge(S["ve"].h, vfin)
                    for tc in range(2):
                        for K in range(16):
                            mm = tensor.matmul(ps[4 + 2 * (ot % 2) + tc][:, 0:512],
                                               bsl(BA_WOS + (ot % 2) * 2048 + K * 128, 128),
                                               ab[:, BA_V + K * 1024 + tc * 512:BA_V + K * 1024 + tc * 512 + 512],
                                               start=(K == 0), stop=(K == 15))
                    S["pe"].inc(mm)

                sec("tensor", f_pe)
                pe0 = S["pe"].v

                def f(vector, ot=ot, pe0=pe0, tokg=tokg):
                    vector.wait_ge(S["pe"].h, pe0)
                    LG.wait(vector, tokg)
                    for tc in range(2):
                        cp = vector.tensor_copy(ab[:, BA_SOUT + tc * 512:BA_SOUT + (tc + 1) * 512],
                                                ps[4 + 2 * (ot % 2) + tc][:, 0:512])
                    ck(vector, "ve", cp)
                    S["ve"].inc(vector.tensor_tensor(
                        ab[:, BA_FIN + ot * 1024:BA_FIN + (ot + 1) * 1024],
                        ab[:, BA_SOUT:BA_SOUT + 1024],
                        ab[:, BA_GP + (ot % 2) * 1024:BA_GP + (ot % 2) * 1024 + 1024], AL.mult))

                sec("vector", f)
                fin_done.append(S["ve"].v)
            v_fin = S["ve"].v
            if "sout" in tap:
                def f(gpsimd, sc=sc, v_fin=v_fin):
                    gpsimd.wait_ge(S["ve"].h, v_fin)
                    STP.issue(gpsimd, 0,
                              tap["sout"][:, sc * QT:(sc + 1) * QT].rearrange("(a p) c -> p a c", p=128),
                              ab[:, BA_FIN:BA_FIN + 16384].rearrange("p (a c) -> p a c", c=1024))

                sec("gpsimd", f)

            add_done = []
            for tb in range(8):
                vstage = S["ve"].v if tb == 0 else None

                BA_FSTG = 34816

                def f(vector, tb=tb, v_fin=v_fin, BA_FSTG=BA_FSTG):
                    if tb == 0:
                        vector.wait_ge(S["ve"].h, v_fin)
                    for dt in range(DT):
                        S["ve"].inc(vector.tensor_copy(
                            bsl(BA_FSTG + dt * 128, 128),
                            ab[:, BA_FIN + dt * 1024 + tb * 128:BA_FIN + dt * 1024 + (tb + 1) * 128]))

                sec("vector", f)
                transpose16(BA_FSTG, BA_DA)
                v1 = S["ve"].v
                so_tok = (0, SO.sems[0].v)
                gate = add_done[tb - 2] if tb >= 2 else None

                def f(sync, sc=sc, tb=tb, gate=gate):
                    if gate is not None:
                        sync.wait_ge(S["ve"].h, gate)
                    LX.issue(sync, tb, fsl(FA_X + (tb % 2) * 2048, 2048),
                             I["xq"][sc * QT + tb * 128: sc * QT + (tb + 1) * 128, :])

                sec("sync", f)
                tok = (tb % 2, LX.sems[tb % 2].v)

                def f(vector, tb=tb, v1=v1, tok=tok, so_tok=so_tok):
                    vector.wait_ge(S["ve"].h, v1)
                    LX.wait(vector, tok)
                    SO.wait(vector, so_tok)
                    S["ve"].inc(vector.tensor_add(fsl(FA_SQ, 2048), fsl(FA_X + (tb % 2) * 2048, 2048),
                                                  bsl(BA_DA, 2048)))

                sec("vector", f)
                add_done.append(S["ve"].v)
                v2 = S["ve"].v

                def f(sync, sc=sc, tb=tb, v2=v2):
                    sync.wait_ge(S["ve"].h, v2)
                    SO.issue(sync, 0, out_o[sc * QT + tb * 128: sc * QT + (tb + 1) * 128, :],
                             fsl(FA_SQ, 2048))

                sec("sync", f)
            fence()

        fence()
        for sc in range(2):
            if "pre" in phases:
                ph_pre(sc)
            if "ffn" in phases:
                ph_ffn_stats(sc)
                fence()
            if "scan" in phases:
                ph_scan(sc)
            if "ffn" in phases:
                ph_ffn_tail(sc)
            if "post" in phases:
                ph_post(sc)

        def f(sync):
            for st in [SY, SB, SQS, SQW, SSC, SG, SO, SPK, STP]:
                st.wait_all(sync)
            sync.wait_ge(S["ag"].h, S["ag"].v)

        sec("sync", f)

    return nc, tap


def _shard_inputs(inputs):
    x = inputs["L0_attn_x"]
    per_core = []
    for c in range(8):
        b, dh = c // 2, c % 2
        xb = np.ascontiguousarray(x[b])
        xq = np.ascontiguousarray(
            np.concatenate([xb[dh * QT:(dh + 1) * QT], xb[TSC + dh * QT:TSC + (dh + 1) * QT]], axis=0))
        dsl = slice(dh * (D // 2), (dh + 1) * (D // 2))
        m = np.zeros((1, 2), np.float32)
        m[0, dh] = 1.0
        per_core.append({
            "xsc": xb,
            "xdh": np.ascontiguousarray(xb[:, dsl]),
            "xq": xq,
            "nw": inputs["L0_attn_norm_w"], "nb": inputs["L0_attn_norm_b"],
            "dprm": inputs["L0_attn_D_param"],
            "nwdh": np.ascontiguousarray(inputs["L0_attn_norm_w"][dsl]),
            "nbdh": np.ascontiguousarray(inputs["L0_attn_norm_b"][dsl]),
            "alogdh": np.ascontiguousarray(inputs["L0_attn_A_log"][dsl]),
            "dpwdh": np.ascontiguousarray(inputs["L0_attn_dt_proj_w"][dsl]),
            "dpbdh": np.ascontiguousarray(inputs["L0_attn_dt_proj_b"][dsl]),
            "xpw": inputs["L0_attn_x_proj_w"],
            "wout": inputs["L0_attn_out_proj_w"],
            "cfc": inputs["L0_ffn_c_fc"], "cpr": inputs["L0_ffn_c_proj"],
            "msel": m,
        })
    return per_core


def kernel(**inputs):
    nc, _ = build_program()
    per_core = _shard_inputs(inputs)
    res = run_bass_kernel_spmd(nc, per_core, core_ids=list(range(8)))
    x = inputs["L0_attn_x"]
    out = np.empty_like(x)
    for c in range(8):
        b, dh = c // 2, c % 2
        o = np.asarray(res.results[c]["o"])
        out[b, dh * QT:(dh + 1) * QT] = o[:QT]
        out[b, TSC + dh * QT:TSC + (dh + 1) * QT] = o[QT:]
    return out


# version 09
# speedup vs baseline: 1.0332x; 1.0332x over previous
import contextlib
import numpy as np
import ml_dtypes
import concourse.bass as bass
import concourse.mybir as mybir
from concourse.bass_utils import run_bass_kernel_spmd

F32 = mybir.dt.float32
BF16 = mybir.dt.bfloat16
AL = mybir.AluOpType
AF = mybir.ActivationFunctionType
AX = mybir.AxisListType
bfd = ml_dtypes.bfloat16

D = 2048
T = 4096
N = 64
R = 128
H = 8192
TSC = 2048
QT = 1024
DT = 16
DHT = 8
EPS = 1e-5
MAGIC = float(np.float32(1.5 * 2 ** 23))
INVD = 1.0 / D


class Sem:
    def __init__(self, h):
        self.h = h
        self.v = 0

    def inc(self, instr, k=1):
        instr.then_inc(self.h, k)
        self.v += k
        return self.v


class Stream:

    def __init__(self, E, nc, name, nslots=2):
        self.sems = [Sem(E(nc.semaphore(f"{name}{i}"))) for i in range(nslots)]
        self.n = nslots

    def issue(self, eng, slot, out, in_, **kw):
        s = self.sems[slot % self.n]
        s.inc(eng.dma_start(out=out, in_=in_, **kw), 16)
        return (slot % self.n, s.v)

    def wait(self, eng, tok):
        slot, v = tok
        eng.wait_ge(self.sems[slot].h, v)

    def wait_all(self, eng):
        for s in self.sems:
            eng.wait_ge(s.h, s.v)


def build_program(taps=(), phases=("pre", "scan", "ffn", "post"), ffn_cut=5):
    nc = bass.Bass("TRN2", target_bir_lowering=False, debug=False)

    I = {}
    for name, shape in [
        ("xsc", [T, D]), ("xdh", [T, D // 2]), ("xq", [2 * QT, D]),
        ("nw", [D]), ("nb", [D]), ("dprm", [D]),
        ("nwdh", [D // 2]), ("nbdh", [D // 2]),
        ("alogdh", [D // 2, N]), ("dpwdh", [D // 2, R]), ("dpbdh", [D // 2]),
        ("xpw", [2 * R, D]), ("wout", [D, D]),
        ("cfc", [D, H]), ("cpr", [D, H]), ("msel", [1, 2]),
    ]:
        I[name] = nc.dram_tensor(name, shape, F32, kind="ExternalInput").ap()

    out_o = nc.dram_tensor("o", [2 * QT, D], F32, kind="ExternalOutput").ap()
    tap = {}
    for tp, shape in dict(
        xn2dh=[DHT * 128, 2 * TSC], dtraw=[R, 2 * TSC], bc=[R, 2 * TSC],
        y=[DHT * 128, 2 * TSC], g=[DT * 128, 2 * QT], vtap=[DT * 128, 2 * QT],
        sout=[DT * 128, 2 * QT], ccol=[128, 128], sc8=[128, 16],
        stats=[128, 64],
    ).items():
        if tp in taps:
            tap[tp] = nc.dram_tensor("tap_" + tp, shape, F32, kind="ExternalOutput").ap()

    d_q1 = nc.dram_tensor("d_q1", [D, H], BF16).ap()
    d_q2T = nc.dram_tensor("d_q2T", [H, D], BF16).ap()
    d_woT = nc.dram_tensor("d_woT", [16, 16, 128, 128], BF16).ap()
    d_bc = nc.dram_tensor("d_bc", [R, TSC], BF16).ap()
    d_c0 = nc.dram_tensor("d_c0", [N, TSC], BF16).ap()
    d_pk = nc.dram_tensor("d_pk", [2, QT], F32).ap()
    d_scal = nc.dram_tensor("d_scal", [1, 8], F32).ap()
    d_sq = nc.dram_tensor("d_sq", [1024, H], BF16).ap()
    d_g = nc.dram_tensor("d_g", [2, DT, 128, QT], BF16).ap()
    d_yin = [nc.dram_tensor(f"d_yin{s}", [DHT * 128, TSC], BF16) for s in range(2)]
    d_yout = [nc.dram_tensor(f"d_yout{s}", [D, TSC], BF16) for s in range(2)]

    with contextlib.ExitStack() as ctx:
        E = ctx.enter_context

        ident16 = E(nc.sbuf_tensor("ident16", [128, 128], BF16))
        a_all = E(nc.sbuf_tensor("a_all", [128, DHT * N], F32))
        ccol = E(nc.sbuf_tensor("ccol", [128, 128], F32))
        states = E(nc.sbuf_tensor("states", [128, DHT * N], F32))
        sc8 = E(nc.sbuf_tensor("sc8", [128, 16], F32))
        stat = E(nc.sbuf_tensor("stat", [128, 64], F32))
        msel_t = E(nc.sbuf_tensor("msel_t", [128, 2], F32))
        cst = E(nc.sbuf_tensor("cst", [128, 2], F32))
        jnk = E(nc.sbuf_tensor("jnk", [128, 4], F32))
        xpwT = E(nc.sbuf_tensor("xpwT", [128, 32 * 128], BF16))
        dpwT = E(nc.sbuf_tensor("dpwT", [128, DHT * 128], BF16))

        AFW = 24576
        ABW = 45056
        af = E(nc.sbuf_tensor("af", [128, AFW], F32))
        ab = E(nc.sbuf_tensor("ab", [128, ABW], BF16))

        ps = [E(nc.psum_tensor(f"ps{i}", [128, 512], F32)) for i in range(8)]

        S = {}
        for nm in ["exp", "h", "g", "y", "yev", "dl", "pe", "ve", "se", "ge", "fh", "fe",
                   "ag", "fen", "dp", "dcp", "ab2", "ln", "q"]:
            S[nm] = Sem(E(nc.semaphore(nm)))

        LV = Stream(E, nc, "lv", 1)
        LX = Stream(E, nc, "lx", 2)
        LW = Stream(E, nc, "lw", 2)
        LW2 = Stream(E, nc, "lw2", 2)
        LW3 = Stream(E, nc, "lw3", 2)
        BCS = Stream(E, nc, "bcs", 2)
        LQ = Stream(E, nc, "lq", 2)
        LQ2 = Stream(E, nc, "lq2", 2)
        LSQ = Stream(E, nc, "lsq", 2)
        LY = Stream(E, nc, "ly", 2)
        LG = Stream(E, nc, "lg", 2)
        LWO = Stream(E, nc, "lwo", 2)
        LPK = Stream(E, nc, "lpk", 1)
        SY = Stream(E, nc, "sy", 1)
        SB = Stream(E, nc, "sb", 1)
        SQS = Stream(E, nc, "sqs", 8)
        SQW = Stream(E, nc, "sqw", 1)
        SSC = Stream(E, nc, "ssc", 1)
        SG = Stream(E, nc, "sg", 1)
        SO = Stream(E, nc, "so", 1)
        SPK = Stream(E, nc, "spk", 1)
        STP = Stream(E, nc, "stp", 1)
        _ALL_STREAMS = [LV, LX, LW, LW2, LW3, BCS, LQ, LQ2, LSQ, LY, LG, LWO, LPK,
                        SY, SB, SQS, SQW, SSC, SG, SO, SPK, STP]

        block = E(nc.Block())

        def fsl(off, w):
            return af[:, off:off + w]

        def bsl(off, w):
            return ab[:, off:off + w]

        FA_X = 0
        FA_SQ = 4096
        FA_XN = 6144
        FA_XNR = 8192
        FA_DL = 10240
        FA_SP = 14336
        FA_FT = 16384
        FA_PK = 18432
        FA_MISC = 19456
        FA_W = 20480

        BA_XN2 = 0
        BA_S16 = 16384
        BA_DTR = 18432
        BA_PRE = 20480
        BA_Y16 = 20480
        BA_BCROW = 22528
        BA_BCB = 24576
        BA_BCC = 28672
        BA_DA = 32768
        BA_U = 36864
        BA_HH = 40960
        BA_FQ1 = 0
        BA_SQR = 0
        BA_FSQS = 41472
        BA_FH1 = 41984
        BA_SILO = 40960
        BA_HQT = 8192
        BA_V = 0
        BA_FIN = 16384
        BA_Y01 = 32768
        BA_SOUT = 36864
        BA_GP = 38912
        BA_WOS = 40960

        def sec(engname, fn):
            getattr(block, engname)(fn)

        def ck(eng, nm, instr):
            v = S[nm].inc(instr)
            eng.wait_ge(S[nm].h, v)

        ALL_STREAMS = _ALL_STREAMS

        def fence():
            S["fen"].v += 5
            tgt = S["fen"].v
            snap = {nm: S[nm].v for nm in S}
            stream_snap = [(st, [s.v for s in st.sems]) for st in ALL_STREAMS]

            def f_sync(eng):
                for st, vals in stream_snap:
                    for s, v in zip(st.sems, vals):
                        eng.wait_ge(s.h, v)
                eng.nop().then_inc(S["fen"].h, 1)
                eng.wait_ge(S["fen"].h, tgt)

            def f_ve(eng):
                for nm in ["ve", "h", "g", "dl", "ab2", "q", "fe"]:
                    eng.wait_ge(S[nm].h, snap[nm])
                eng.nop().then_inc(S["fen"].h, 1)
                eng.wait_ge(S["fen"].h, tgt)

            def f_se(eng):
                for nm in ["se", "exp", "dcp", "ln", "yev", "fh"]:
                    eng.wait_ge(S[nm].h, snap[nm])
                eng.nop().then_inc(S["fen"].h, 1)
                eng.wait_ge(S["fen"].h, tgt)

            def f_pe(eng):
                for nm in ["pe", "y", "dp"]:
                    eng.wait_ge(S[nm].h, snap[nm])
                eng.nop().then_inc(S["fen"].h, 1)
                eng.wait_ge(S["fen"].h, tgt)

            def f_gp(eng):
                eng.wait_ge(S["ge"].h, snap["ge"])
                eng.nop().then_inc(S["fen"].h, 1)
                eng.wait_ge(S["fen"].h, tgt)

            sec("sync", f_sync)
            sec("vector", f_ve)
            sec("scalar", f_se)
            sec("tensor", f_pe)
            sec("gpsimd", f_gp)

        def transpose16(src_off, dst_off, nblk=16):
            for r0 in range(0, nblk, 8):
                nb = min(8, nblk - r0)
                ve0 = S["ve"].v

                def f_pe(tensor, r0=r0, nb=nb, ve0=ve0):
                    tensor.wait_ge(S["ve"].h, ve0)
                    for j in range(nb):
                        tr = tensor.transpose(ps[j][:, 0:64].bitcast(BF16),
                                              bsl(src_off + (r0 + j) * 128, 128), ident16[:])
                    S["pe"].inc(tr)

                sec("tensor", f_pe)
                pe0 = S["pe"].v

                def f_ve(vector, r0=r0, nb=nb, pe0=pe0):
                    vector.wait_ge(S["pe"].h, pe0)
                    for j in range(nb):
                        S["ve"].inc(vector.tensor_copy(bsl(dst_off + (r0 + j) * 128, 128),
                                                       ps[j][:, 0:64].bitcast(BF16)))

                sec("vector", f_ve)

        def tap_store(name, src_ap, eng="gpsimd"):
            if name not in tap:
                return

            def f(e, src_ap=src_ap, name=name):
                e.wait_ge(S["ve"].h, S["ve"].v)
                e.wait_ge(S["se"].h, S["se"].v)
                STP.issue(e, 0, tap[name], src_ap)

            sec(eng, f)

        nc.const_aps.register(F32, EPS, cst[:, 0:1])
        nc.const_aps.register(F32, -0.5, cst[:, 1:2])

        def f_gp(gpsimd):
            S["ge"].inc(gpsimd.memset(cst[:, 0:1], EPS))
            S["ge"].inc(gpsimd.memset(cst[:, 1:2], -0.5))
            S["ge"].inc(gpsimd.memset(ident16[:], 0.0))
            S["ge"].inc(gpsimd.memset(states[:], 0.0))
            S["ge"].inc(gpsimd.memset(stat[:], 0.0))
            S["ge"].inc(gpsimd.memset(ccol[:], 0.0))
            S["ge"].inc(gpsimd.memset(sc8[:], 0.0))
            S["ge"].inc(gpsimd.memset(jnk[:], 0.0))
            gpsimd.wait_ge(S["ge"].h, S["ge"].v)
            S["ge"].inc(gpsimd.affine_select(out=ident16[:], in_=ident16[:],
                                             compare_op=AL.not_equal, fill=1.0, base=0,
                                             pattern=[[-1, 128]], channel_multiplier=1))

        sec("gpsimd", f_gp)
        ge0 = S["ge"].v

        def f_ld(sync):
            sync.wait_ge(S["ge"].h, ge0)
            for nm, base, cnt in [("nw", 0, 16), ("nb", 16, 16), ("dprm", 32, 16),
                                  ("dpbdh", 48, 8), ("nwdh", 56, 8), ("nbdh", 64, 8)]:
                LV.issue(sync, 0, ccol[:, base:base + cnt],
                         I[nm].rearrange("(a p) -> p a", p=128),
                         allow_slow_non_contiguous=True)
            for dt in range(DHT):
                LV.issue(sync, 0, a_all[:, dt * N:(dt + 1) * N],
                         I["alogdh"][dt * 128:(dt + 1) * 128, :])
            LV.issue(sync, 0, msel_t[:, :], I["msel"].partition_broadcast(128))

        sec("sync", f_ld)

        def f_se(scalar):
            LV.wait_all(scalar)
            S["se"].inc(scalar.activation(a_all[:], a_all[:], AF.Exp))

        sec("scalar", f_se)
        se_a = S["se"].v

        def f_ve(vector):
            vector.wait_ge(S["se"].h, se_a)
            S["ve"].inc(vector.tensor_scalar_mul(a_all[:], a_all[:], -1.0))

        sec("vector", f_ve)

        for ft in range(2):
            vgate = S["ve"].v

            def f(sync, ft=ft, vgate=vgate):
                sync.wait_ge(S["ve"].h, vgate)
                LX.issue(sync, ft, fsl(FA_X + (ft % 2) * 2048, 2048),
                         I["xpw"][ft * 128:(ft + 1) * 128, :])

            sec("sync", f)
            tok = (ft % 2, LX.sems[ft % 2].v)

            def f(vector, tok=tok, ft=ft):
                LX.wait(vector, tok)
                S["ve"].inc(vector.tensor_copy(bsl(BA_PRE, 2048), fsl(FA_X + (ft % 2) * 2048, 2048)))

            sec("vector", f)
            transpose16(BA_PRE, BA_DA)
            ve1 = S["ve"].v

            def f(vector, ft=ft, ve1=ve1):
                vector.wait_ge(S["ve"].h, ve1)
                for K in range(16):
                    S["ve"].inc(vector.tensor_copy(
                        xpwT[:, (K * 2 + ft) * 128:(K * 2 + ft) * 128 + 128],
                        bsl(BA_DA + K * 128, 128)))

            sec("vector", f)

        vgate = S["ve"].v

        def f(sync, vgate=vgate):
            sync.wait_ge(S["ve"].h, vgate)
            LX.issue(sync, 0, fsl(FA_X, 1024).rearrange("p (a r) -> p a r", r=128),
                     I["dpwdh"].rearrange("(a p) r -> p a r", p=128))

        sec("sync", f)
        tok = (0, LX.sems[0].v)

        def f(vector, tok=tok):
            LX.wait(vector, tok)
            S["ve"].inc(vector.tensor_copy(bsl(BA_PRE, 1024), fsl(FA_X, 1024)))

        sec("vector", f)
        transpose16(BA_PRE, BA_DA, nblk=8)
        ve1 = S["ve"].v

        def f(vector, ve1=ve1):
            vector.wait_ge(S["ve"].h, ve1)
            S["ve"].inc(vector.tensor_copy(dpwT[:, :], bsl(BA_DA, 1024)))

        sec("vector", f)
        fence()

        for ot in range(16):
            vgate = S["ve"].v
            sgate = (0, SQW.sems[0].v)

            def f(sync, ot=ot, vgate=vgate):
                sync.wait_ge(S["ve"].h, vgate)
                LX.issue(sync, ot, fsl(FA_X + (ot % 2) * 2048, 2048),
                         I["wout"][ot * 128:(ot + 1) * 128, :])

            sec("sync", f)
            tok = (ot % 2, LX.sems[ot % 2].v)

            def f(vector, tok=tok, ot=ot, sgate=sgate):
                LX.wait(vector, tok)
                SQW.wait(vector, sgate)
                S["ve"].inc(vector.tensor_copy(bsl(BA_PRE, 2048), fsl(FA_X + (ot % 2) * 2048, 2048)))

            sec("vector", f)
            transpose16(BA_PRE, BA_DA)
            ve1 = S["ve"].v

            def f(sync, ot=ot, ve1=ve1):
                sync.wait_ge(S["ve"].h, ve1)
                SQW.issue(sync, 0, d_woT[ot, :, :, :].rearrange("a p c -> p a c"),
                          ab[:, BA_DA:BA_DA + 2048].rearrange("p (a c) -> p a c", c=128))

            sec("sync", f)
        fence()
        tap_store("ccol", ccol[:, :])

        W_STREAMS = [LW, LW2, LW3]
        W_ENGS = ["sync", "scalar", "gpsimd"]
        W_BUFS = [FA_W, FA_X, FA_DL]

        def wbuf(it):
            q = it % 3
            slot = (it // 3) % 2
            return fsl(W_BUFS[q] + slot * 2048, 2048)

        def wload(it, wname, dt, ch, gate):
            q = it % 3

            def f(eng, wname=wname, dt=dt, ch=ch, gate=gate, it=it, q=q):
                if gate is not None:
                    eng.wait_ge(S["se"].h, gate)
                W_STREAMS[q].issue(eng, it // 3, wbuf(it),
                                   I[wname][dt * 128:(dt + 1) * 128, ch * 2048:(ch + 1) * 2048])

            sec(W_ENGS[q], f)
            slot = (it // 3) % 2
            return (slot, W_STREAMS[q].sems[slot].v)

        b_seA = S["se"].v
        toksA = {}
        for wi, wname in enumerate(["cfc", "cpr"]):
            for dt in range(DT):
                for ch in range(4):
                    it = wi * 64 + dt * 4 + ch
                    q = it % 3
                    if q == 1:
                        continue

                    def f(eng, wname=wname, dt=dt, ch=ch, it=it, q=q):
                        if it >= 6:
                            eng.wait_ge(S["se"].h, b_seA + it - 5)
                        W_STREAMS[q].issue(eng, it // 3, wbuf(it),
                                           I[wname][dt * 128:(dt + 1) * 128,
                                                    ch * 2048:(ch + 1) * 2048])

                    sec(W_ENGS[q], f)
                    slot = (it // 3) % 2
                    toksA[it] = (slot, W_STREAMS[q].sems[slot].v)

        def f(scalar, toksA=toksA):
            def q1i(jt):
                wname = "cfc" if jt < 64 else "cpr"
                dtj = (jt % 64) // 4
                chj = jt % 4
                W_STREAMS[1].issue(scalar, jt // 3, wbuf(jt),
                                   I[wname][dtj * 128:(dtj + 1) * 128,
                                            chj * 2048:(chj + 1) * 2048])
                toksA[jt] = ((jt // 3) % 2, W_STREAMS[1].sems[(jt // 3) % 2].v)

            q1i(1)
            q1i(4)
            for it in range(128):
                W_STREAMS[it % 3].wait(scalar, toksA[it])
                ck(scalar, "se", scalar.activation(fsl(FA_SQ, 2048),
                                                   wbuf(it),
                                                   AF.Abs,
                                                   accum_out=af[:, FA_XN + it:FA_XN + it + 1]))
                if it + 6 < 128 and (it + 6) % 3 == 1:
                    q1i(it + 6)

        sec("scalar", f)

        def f(vector):
            vector.wait_ge(S["se"].h, S["se"].v)
            ck(vector, "ve", vector.tensor_reduce(stat[:, 8:9],
                                                  af[:, FA_XN:FA_XN + 64], AX.X, AL.add))
            S["ve"].inc(vector.tensor_reduce(stat[:, 9:10],
                                             af[:, FA_XN + 64:FA_XN + 128], AX.X, AL.add))

        sec("vector", f)
        ve_t = S["ve"].v

        def f(sync, ve_t=ve_t):
            sync.wait_ge(S["ve"].h, ve_t)
            SSC.issue(sync, 0, d_sq[0:128, 0:4].bitcast(F32), stat[:, 8:10])
            SSC.wait_all(sync)
            LV.issue(sync, 0, fsl(FA_MISC, 256)[0:1, 0:128],
                     d_sq[0:128, 0:2].bitcast(F32).rearrange("p o -> o p"),
                     allow_slow_non_contiguous=True)
            LV.issue(sync, 0, fsl(FA_MISC, 256)[0:1, 128:256],
                     d_sq[0:128, 2:4].bitcast(F32).rearrange("p o -> o p"),
                     allow_slow_non_contiguous=True)

        sec("sync", f)

        def f(vector):
            LV.wait_all(vector)
            vector.tensor_reduce(fsl(FA_MISC, 512)[0:1, 256:257], fsl(FA_MISC, 512)[0:1, 0:128],
                                 AX.X, AL.add)
            S["ve"].inc(vector.tensor_reduce(fsl(FA_MISC, 512)[0:1, 257:258],
                                             fsl(FA_MISC, 512)[0:1, 128:256], AX.X, AL.add))

        sec("vector", f)
        ve_t2 = S["ve"].v

        def f(sync, ve_t2=ve_t2):
            sync.wait_ge(S["ve"].h, ve_t2)
            SSC.issue(sync, 0, d_scal[0:1, 0:2], fsl(FA_MISC, 512)[0:1, 256:258])
            SSC.wait_all(sync)
            LV.issue(sync, 0, sc8[:, 0:2], d_scal[0:1, 0:2].partition_broadcast(128))

        sec("sync", f)

        def f(vector):
            LV.wait_all(vector)
            ck(vector, "ve", vector.tensor_scalar(sc8[:, 0:2], sc8[:, 0:2],
                                                  1.0 / (D * H), EPS, AL.mult, AL.max))
            vector.reciprocal(sc8[:, 2:4], sc8[:, 0:2])
            vector.tensor_mul(sc8[:, 4:5], sc8[:, 0:1], sc8[:, 0:1])
            S["ve"].inc(vector.tensor_scalar_mul(sc8[:, 5:6], sc8[:, 1:2], 1.0 / 255.0))

        sec("vector", f)
        fence()
        tap_store("sc8", sc8[:, :])

        def sqb(it):
            return fsl(FA_SQ if it % 2 == 0 else FA_XNR, 2048)

        se_gate = []
        ve_sq = []
        for wi, wname in enumerate(["cfc", "cpr"]):
            for dt in range(DT):
                for ch in range(4):
                    it = wi * 64 + dt * 4 + ch
                    gate = se_gate[it - 6] if it >= 6 else None
                    st_tok = (0, SQW.sems[0].v)
                    tok = wload(it, wname, dt, ch, gate)
                    v_prev = ve_sq[it - 2] if it >= 2 else S["ve"].v

                    def f(scalar, wi=wi, it=it, tok=tok, v_prev=v_prev):
                        W_STREAMS[it % 3].wait(scalar, tok)
                        scalar.wait_ge(S["ve"].h, v_prev)
                        S["se"].inc(scalar.activation(sqb(it),
                                                      wbuf(it),
                                                      AF.Copy, scale=sc8[:, 2 + wi:3 + wi]))

                    sec("scalar", f)
                    s0 = S["se"].v
                    se_gate.append(s0)

                    def f(vector, it=it, s0=s0, st_tok=st_tok):
                        vector.wait_ge(S["se"].h, s0)
                        ck(vector, "ve", vector.tensor_scalar(sqb(it), sqb(it),
                                                              1.0, -1.0, AL.min, AL.max))
                        SQW.wait(vector, st_tok)
                        S["ve"].inc(vector.tensor_scalar(bsl(BA_PRE, 2048), sqb(it),
                                                         MAGIC, MAGIC, AL.add, AL.subtract))

                    sec("vector", f)
                    ve_sq.append(S["ve"].v)
                    ve1 = S["ve"].v
                    if wi == 0:
                        def f(sync, dt=dt, ch=ch, ve1=ve1):
                            sync.wait_ge(S["ve"].h, ve1)
                            SQW.issue(sync, 0,
                                      d_q1[dt * 128:(dt + 1) * 128, ch * 2048:(ch + 1) * 2048],
                                      bsl(BA_PRE, 2048))

                        sec("sync", f)
                    else:
                        transpose16(BA_PRE, BA_DA)
                        ve2 = S["ve"].v

                        def f(sync, dt=dt, ch=ch, ve2=ve2):
                            sync.wait_ge(S["ve"].h, ve2)
                            SQW.issue(sync, 0,
                                      d_q2T[ch * 2048:(ch + 1) * 2048, dt * 128:(dt + 1) * 128]
                                      .rearrange("(a p) c -> p a c", p=128),
                                      ab[:, BA_DA:BA_DA + 2048].rearrange("p (a c) -> p a c", c=128))

                        sec("sync", f)
        fence()

        def ln_stats(xoff, w, so, invw):

            s_entry = S["se"].v

            def f(scalar, xoff=xoff, w=w, s_entry=s_entry):
                scalar.wait_ge(S["se"].h, s_entry)
                ck(scalar, "se", scalar.activation(fsl(FA_SQ, w), fsl(xoff, w), AF.Copy,
                                                   accum_out=stat[:, 0:1]))
                S["se"].inc(scalar.activation(fsl(FA_SQ, w), fsl(xoff, w), AF.Square,
                                              accum_out=stat[:, 1:2]))

            sec("scalar", f)
            s1 = S["se"].v

            def f(vector, s1=s1, invw=invw):
                vector.wait_ge(S["se"].h, s1)
                ck(vector, "ve", vector.tensor_scalar_mul(stat[:, 2:3], stat[:, 0:1], invw))
                vector.tensor_mul(stat[:, 3:4], stat[:, 2:3], stat[:, 2:3])
                ck(vector, "ve", vector.tensor_scalar_mul(stat[:, 4:5], stat[:, 1:2], invw))
                S["ve"].inc(vector.tensor_sub(stat[:, 5:6], stat[:, 4:5], stat[:, 3:4]))

            sec("vector", f)
            v1 = S["ve"].v

            def f(scalar, so=so, v1=v1):
                scalar.wait_ge(S["ve"].h, v1)
                ck(scalar, "se", scalar.activation(stat[:, 6:7], stat[:, 5:6], AF.Ln, bias=EPS))
                ck(scalar, "se", scalar.activation(stat[:, so:so + 1], stat[:, 6:7],
                                                   AF.Exp, scale=-0.5))
                ck(scalar, "se", scalar.activation(stat[:, 7:8], stat[:, 2:3], AF.Copy,
                                                   scale=stat[:, so:so + 1]))
                ck(scalar, "se", scalar.activation(stat[:, so + 1:so + 2], stat[:, 7:8],
                                                   AF.Copy, scale=-1.0))
                S["se"].inc(scalar.nop())

            sec("scalar", f)

        def ph_pre(sc):
            se_xfree = []
            for tb in range(16):
                gate = se_xfree[tb - 2] if tb >= 2 else None

                def f(sync, sc=sc, tb=tb, gate=gate):
                    if gate is not None:
                        sync.wait_ge(S["se"].h, gate)
                    LX.issue(sync, tb, fsl(FA_X + (tb % 2) * 2048, 2048),
                             I["xsc"][sc * TSC + tb * 128: sc * TSC + (tb + 1) * 128, :])

                sec("sync", f)
                tok = (tb % 2, LX.sems[tb % 2].v)

                def f(scalar, tok=tok):
                    LX.wait(scalar, tok)

                sec("scalar", f)
                ln_stats(FA_X + (tb % 2) * 2048, 2048, 8, INVD)

                def f(scalar, tb=tb):
                    S["se"].inc(scalar.activation(fsl(FA_XN, 2048), fsl(FA_X + (tb % 2) * 2048, 2048),
                                                  AF.Identity, scale=stat[:, 8:9], bias=stat[:, 9:10]))

                sec("scalar", f)
                se_xfree.append(S["se"].v)
                ln_stats(FA_XN, 2048, 10, INVD)

                def f(scalar, tb=tb):
                    S["se"].inc(scalar.activation(fsl(FA_XNR, 2048), fsl(FA_XN, 2048),
                                                  AF.Identity, scale=stat[:, 10:11], bias=stat[:, 11:12]))

                sec("scalar", f)
                s2 = S["se"].v

                def f(vector, tb=tb, s2=s2):
                    vector.wait_ge(S["se"].h, s2)
                    vector.tensor_mul(stat[:, 16 + tb:17 + tb], stat[:, 8:9], stat[:, 10:11])
                    ck(vector, "ve", vector.tensor_mul(stat[:, 12:13], stat[:, 9:10], stat[:, 10:11]))
                    vector.tensor_add(stat[:, 32 + tb:33 + tb], stat[:, 12:13], stat[:, 11:12])
                    S["ve"].inc(vector.tensor_copy(bsl(BA_PRE, 2048), fsl(FA_XNR, 2048)))

                sec("vector", f)
                transpose16(BA_PRE, BA_DA)
                v2 = S["ve"].v

                def f(vector, v2=v2):
                    vector.wait_ge(S["ve"].h, v2)
                    for dt in range(DT):
                        S["ve"].inc(vector.tensor_scalar(
                            bsl(BA_DA + dt * 128, 128), bsl(BA_DA + dt * 128, 128),
                            ccol[:, dt:dt + 1], ccol[:, 16 + dt:17 + dt], AL.mult, AL.add))

                sec("vector", f)
                v3 = S["ve"].v

                def f(tensor, tb=tb, v3=v3):
                    tensor.wait_ge(S["ve"].h, v3)
                    for ft in range(2):
                        for K in range(16):
                            st_ap = xpwT[:, (K * 2 + ft) * 128:(K * 2 + ft) * 128 + 128]
                            mm = tensor.matmul(ps[4 + ft][:, 0:128], st_ap,
                                               bsl(BA_DA + K * 128, 128), start=(K == 0), stop=(K == 15))
                    S["pe"].inc(mm)

                sec("tensor", f)
                pe1 = S["pe"].v

                def f(vector, tb=tb, pe1=pe1):
                    vector.wait_ge(S["pe"].h, pe1)
                    vector.tensor_copy(ab[:, BA_DTR + tb * 128:BA_DTR + (tb + 1) * 128], ps[4][:, 0:128])
                    S["ve"].inc(vector.tensor_copy(ab[:, BA_BCROW + tb * 128:BA_BCROW + (tb + 1) * 128],
                                                   ps[5][:, 0:128]))

                sec("vector", f)
            vbc = S["ve"].v

            def f(sync, sc=sc, vbc=vbc):
                sync.wait_ge(S["ve"].h, vbc)
                SB.issue(sync, 0, d_bc[:, :], bsl(BA_BCROW, 2048))
                if sc == 0:
                    SB.wait_all(sync)
                    for rep in range(4):
                        SB.issue(sync, 0, d_c0[:, rep * 512:(rep + 1) * 512],
                                 ab[64:128, BA_BCROW:BA_BCROW + 512])
                        SB.wait_all(sync)

            sec("sync", f)

            se_dhfree = []
            ve_xnread = []
            v_affine = S["ve"].v
            for tb in range(16):
                gate = se_dhfree[tb - 2] if tb >= 2 else None

                def f(sync, sc=sc, tb=tb, gate=gate):
                    if gate is not None:
                        sync.wait_ge(S["se"].h, gate)
                    LX.issue(sync, tb, fsl(FA_X + (tb % 2) * 1024, 1024),
                             I["xdh"][sc * TSC + tb * 128: sc * TSC + (tb + 1) * 128, :])

                sec("sync", f)
                tok = (tb % 2, LX.sems[tb % 2].v)

                vgate = ve_xnread[tb - 1] if tb >= 1 else v_affine

                def f(scalar, tb=tb, tok=tok, vgate=vgate):
                    LX.wait(scalar, tok)
                    scalar.wait_ge(S["ve"].h, vgate)
                    S["se"].inc(scalar.activation(fsl(FA_XN, 1024), fsl(FA_X + (tb % 2) * 1024, 1024),
                                                  AF.Identity, scale=stat[:, 16 + tb:17 + tb],
                                                  bias=stat[:, 32 + tb:33 + tb]))

                sec("scalar", f)
                se_dhfree.append(S["se"].v)
                s0 = S["se"].v

                def f(vector, s0=s0):
                    vector.wait_ge(S["se"].h, s0)
                    S["ve"].inc(vector.tensor_copy(bsl(BA_PRE, 1024), fsl(FA_XN, 1024)))

                sec("vector", f)
                ve_xnread.append(S["ve"].v)
                transpose16(BA_PRE, BA_DA, nblk=8)
                v1 = S["ve"].v

                def f(vector, tb=tb, v1=v1):
                    vector.wait_ge(S["ve"].h, v1)
                    for dt in range(DHT):
                        S["ve"].inc(vector.tensor_scalar(
                            ab[:, BA_XN2 + dt * 2048 + tb * 128: BA_XN2 + dt * 2048 + (tb + 1) * 128],
                            bsl(BA_DA + dt * 128, 128),
                            ccol[:, 56 + dt:57 + dt], ccol[:, 64 + dt:65 + dt], AL.mult, AL.add))

                sec("vector", f)
            fence()
            if "xn2dh" in tap:
                def f(gpsimd, sc=sc):
                    STP.issue(gpsimd, 0,
                              tap["xn2dh"][:, sc * TSC:(sc + 1) * TSC].rearrange("(a p) c -> p a c", p=128),
                              ab[:, BA_XN2:BA_XN2 + 16384].rearrange("p (a c) -> p a c", c=2048))

                sec("gpsimd", f)
            if "dtraw" in tap:
                def f(gpsimd, sc=sc):
                    STP.issue(gpsimd, 0, tap["dtraw"][:, sc * TSC:(sc + 1) * TSC], bsl(BA_DTR, 2048))

                sec("gpsimd", f)
            if "bc" in tap:
                def f(gpsimd, sc=sc):
                    STP.issue(gpsimd, 0, tap["bc"][:, sc * TSC:(sc + 1) * TSC], bsl(BA_BCROW, 2048))

                sec("gpsimd", f)
            if "stats" in tap and sc == 0:
                def f(gpsimd):
                    gpsimd.wait_ge(S["ve"].h, S["ve"].v)
                    STP.issue(gpsimd, 0, tap["stats"], stat[:, :])

                sec("gpsimd", f)
            fence()

        def ph_scan(sc):
            NI = DHT * N
            b_exp = S["exp"].v
            b_h = S["h"].v
            b_g = S["g"].v
            b_y = S["y"].v
            b_yev = S["yev"].v
            b_dl = S["dl"].v
            b_dp = S["dp"].v
            b_dcp = S["dcp"].v
            b_ab = S["ab2"].v
            b_ln = S["ln"].v
            b_bc = [BCS.sems[0].v, BCS.sems[1].v]
            b_sy = SY.sems[0].v
            b_pe = S["pe"].v
            b_lq = [LQ.sems[0].v, LQ.sems[1].v]
            b_sqs = [s.v for s in SQS.sems]
            b_fh = S["fh"].v
            b_fe = S["fe"].v

            do_h1 = "ffn" in phases

            def fe_stt(it):
                return b_fe + 3 * it + 1 + (2 if it >= 128 else 0)

            def bank(dt, k):
                return ps[(dt % 2) * 4 + k]

            def hbank(c, j):
                return ps[(c % 2) * 4 + (j % 2)]

            def h1_copy(scalar, it):
                c = it // 32
                j = it % 4
                scalar.wait_ge(S["pe"].h, b_pe + it + 1)
                if it >= 32:
                    scalar.wait_ge(S["fe"].h, fe_stt(it - 32))
                scalar.activation(fh1v(it % 32), hbank(c, j)[:, 0:256],
                                  AF.Copy).then_inc(S["fh"].h, 1)
                if it % 4 == 3 and it // 4 + 2 < 64:
                    lq_issue(scalar, it // 4 + 2)

            def evac_chunk(vector, c):
                for k32 in range(32):
                    it = c * 32 + k32
                    j = it % 4
                    vector.wait_ge(S["fh"].h, b_fh + it + 1)
                    if it >= 8:
                        q8 = (it - 8) % 8
                        vector.wait_ge(SQS.sems[q8].h, b_sqs[q8] + 16 * ((it - 8) // 8 + 1))
                    ck(vector, "fe", vector.scalar_tensor_tensor(
                        fsqv(it % 8), fh1v(it % 32), 0.0, fh1v(it % 32), AL.max, AL.mult))
                    ck(vector, "fe", vector.tensor_reduce(
                        stat[:, 12:13], fsqv(it % 8), AX.X, AL.max))
                    ck(vector, "fe", vector.tensor_tensor(stat[:, 8 + j:9 + j],
                                                          stat[:, 8 + j:9 + j],
                                                          stat[:, 12:13], AL.max))
                if c == 3:
                    ck(vector, "fe", vector.tensor_copy(stat[:, 16:20], stat[:, 8:12]))
                    ck(vector, "fe", vector.memset(stat[:, 8:12], 0.0))
                    fe_mark[(sc, 0)] = S["fe"].v
                if c == DHT - 1:
                    fe_mark[(sc, 1)] = S["fe"].v

            def f(vector):
                vector.wait_ge(S["se"].h, S["se"].v)
                S["ve"].inc(vector.memset(stat[:, 8:12], 0.0))

            sec("vector", f)
            v_init = S["ve"].v

            def lq_issue(eng, hc):
                LQ.issue(eng, hc, q1v(hc % 2).rearrange("p (a c) -> p a c", c=256),
                         d_q1[:, (hc % 32) * 256:(hc % 32 + 1) * 256]
                         .rearrange("(a p) c -> p a c", p=128))

            def f_sync(sync, sc=sc):
                SB.wait_all(sync)
                for dt in range(DHT):
                    for n in range(N):
                        i = dt * N + n
                        p = i % 2
                        if i >= 2:
                            sync.wait_ge(S["g"].h, b_g + i - 1)
                        BCS.issue(sync, p, bsl(BA_BCB + p * 2048, 2048),
                                  d_bc[n:n + 1, :].partition_broadcast(128))
                        BCS.sems[p].inc(sync.dma_start(out=bsl(BA_BCC + p * 2048, 2048),
                                                       in_=d_c0[n:n + 1, :].partition_broadcast(128)), 16)

            sec("sync", f_sync)

            def h1_group(tensor, it):
                c = it // 32
                hc = it // 4
                quad = hc // 32
                j = it % 4
                if it % 32 == 0:
                    tensor.wait_ge(S["yev"].h, b_yev + c + 1)
                if j == 0:
                    tensor.wait_ge(LQ.sems[hc % 2].h, b_lq[hc % 2] + 16 * (hc // 2 + 1))
                if it >= 2:
                    tensor.wait_ge(S["fh"].h, b_fh + it - 1)
                else:
                    tensor.wait_ge(S["ve"].h, v_init)
                for K in range(16):
                    mm = tensor.matmul(hbank(c, j)[:, 0:256],
                                       xnqv(quad)[:, K * 512 + j * 128:K * 512 + j * 128 + 128],
                                       q1v(hc % 2)[:, K * 256:(K + 1) * 256],
                                       start=(K == 0), stop=(K == 15))
                mm.then_inc(S["pe"].h, 1)

            def f_pe(tensor, sc=sc):
                for dt in range(DHT):
                    if dt >= 2:
                        tensor.wait_ge(S["yev"].h, b_yev + dt - 1)
                        if do_h1:
                            tensor.wait_ge(S["fh"].h, b_fh + 32 * (dt - 1))
                    for k in range(4):
                        mm = tensor.matmul(bank(dt, k)[:, :], dpwT[:, dt * 128:(dt + 1) * 128],
                                           bsl(BA_DTR + k * 512, 512), start=True, stop=True)
                    mm.then_inc(S["dp"].h, 1)
                    for n in range(N):
                        i = dt * N + n
                        tensor.wait_ge(S["g"].h, b_g + i + 1)
                        for k in range(4):
                            mm = tensor.matmul(bank(dt, k)[:, :], ident16[:],
                                               bsl(BA_HH + (i % 2) * 2048 + k * 512, 512),
                                               start=(n == 0), stop=(n == N - 1))
                        mm.then_inc(S["y"].h, 1)
                        if do_h1 and dt >= 1 and n % 2 == 1:
                            h1_group(tensor, 32 * (dt - 1) + n // 2)
                for it in (range(32 * (DHT - 1), 32 * DHT) if do_h1 else ()):
                    h1_group(tensor, it)
                S["dp"].v += DHT
                S["y"].v += NI
                if do_h1:
                    S["pe"].v += 256

            sec("tensor", f_pe)

            def f_se(scalar, sc=sc):
                if do_h1:
                    lq_issue(scalar, 0)
                    lq_issue(scalar, 1)
                for dt in range(DHT):
                    scalar.wait_ge(S["dp"].h, b_dp + dt + 1)
                    dloff = FA_DL + (dt % 2) * 2048
                    for k in range(4):
                        ac = scalar.activation(fsl(dloff + k * 512, 512), bank(dt, k)[:, :],
                                               AF.Identity, bias=ccol[:, 48 + dt:49 + dt])
                    ac.then_inc(S["dcp"].h, 1)
                    scalar.wait_ge(S["ab2"].h, b_ab + dt + 1)
                    ck(scalar, "se", scalar.activation(fsl(FA_SP, 2048), fsl(FA_SP, 2048),
                                                       AF.Exp, scale=-1.0))
                    scalar.activation(fsl(FA_SP, 2048), fsl(FA_SP, 2048), AF.Ln, bias=1.0
                                      ).then_inc(S["ln"].h, 1)
                    scalar.wait_ge(S["dl"].h, b_dl + dt + 1)
                    for n in range(N):
                        i = dt * N + n
                        if i >= 2:
                            scalar.wait_ge(S["h"].h, b_h + i - 1)
                        scalar.activation(bsl(BA_DA + (i % 2) * 2048, 2048), fsl(dloff, 2048),
                                          AF.Exp, scale=a_all[:, dt * N + n:dt * N + n + 1]
                                          ).then_inc(S["exp"].h, 1)
                        if do_h1 and dt >= 1 and n % 2 == 0 and n >= 2:
                            h1_copy(scalar, 32 * (dt - 1) + n // 2 - 1)
                    if do_h1 and dt >= 1:
                        h1_copy(scalar, 32 * (dt - 1) + 31)
                    scalar.wait_ge(S["y"].h, b_y + (dt + 1) * N)
                    scalar.wait_ge(SY.sems[0].h, b_sy + dt * 16)
                    for k in range(4):
                        ac = scalar.activation(bsl(BA_Y16 + k * 512, 512), bank(dt, k)[:, :], AF.Copy)
                    ac.then_inc(S["yev"].h, 1)
                for it in (range(32 * (DHT - 1), 32 * DHT) if do_h1 else ()):
                    h1_copy(scalar, it)
                S["dcp"].v += DHT
                S["ln"].v += DHT
                S["exp"].v += NI
                S["yev"].v += DHT
                if do_h1:
                    S["fh"].v += 256

            sec("scalar", f_se)

            def f_ve(vector, sc=sc):
                for dt in range(DHT):
                    dloff = FA_DL + (dt % 2) * 2048
                    vector.wait_ge(S["dcp"].h, b_dcp + dt + 1)
                    vector.scalar_tensor_tensor(fsl(FA_SP, 2048), fsl(dloff, 2048), -1.0,
                                                fsl(dloff, 2048), AL.mult, AL.max
                                                ).then_inc(S["ab2"].h, 1)
                    vector.wait_ge(S["ln"].h, b_ln + dt + 1)
                    ck(vector, "ve", vector.scalar_tensor_tensor(fsl(dloff, 2048), fsl(dloff, 2048),
                                                                 0.0, fsl(FA_SP, 2048),
                                                                 AL.max, AL.add))
                    vector.tensor_tensor(bsl(BA_S16, 2048), fsl(dloff, 2048),
                                         bsl(BA_XN2 + dt * 2048, 2048), AL.mult
                                         ).then_inc(S["dl"].h, 1)
                    for n in range(N):
                        i = dt * N + n
                        p = i % 2
                        vector.wait_ge(BCS.sems[p].h, b_bc[p] + 32 * (i // 2 + 1))
                        vector.wait_ge(S["exp"].h, b_exp + i + 1)
                        if i >= 2:
                            vector.wait_ge(S["y"].h, b_y + i - 1)
                        ck(vector, "ve", vector.tensor_tensor(bsl(BA_U + p * 2048, 2048),
                                                               bsl(BA_S16, 2048),
                                                               bsl(BA_BCB + p * 2048, 2048), AL.mult))
                        ck(vector, "ve", vector.tensor_tensor_scan(bsl(BA_HH + p * 2048, 2048),
                                                                   bsl(BA_DA + p * 2048, 2048),
                                                                   bsl(BA_U + p * 2048, 2048),
                                                                   states[:, dt * N + n:dt * N + n + 1],
                                                                   AL.mult, AL.add))
                        vector.tensor_copy(states[:, dt * N + n:dt * N + n + 1],
                                           bsl(BA_HH + p * 2048, 2048)[:, 2047:2048]
                                           ).then_inc(S["h"].h, 1)
                        vector.wait_ge(S["h"].h, b_h + i + 1)
                        vector.tensor_tensor(bsl(BA_HH + p * 2048, 2048), bsl(BA_HH + p * 2048, 2048),
                                             bsl(BA_BCC + p * 2048, 2048), AL.mult
                                             ).then_inc(S["g"].h, 1)
                    if do_h1 and dt >= 1:
                        evac_chunk(vector, dt - 1)
                if do_h1:
                    evac_chunk(vector, DHT - 1)
                S["ab2"].v += DHT
                S["dl"].v += DHT
                S["h"].v += NI
                S["g"].v += NI

            sec("vector", f_ve)

            def sqs_chunk(gpsimd, c):
                for k32 in range(32):
                    it = c * 32 + k32
                    quad = it // 128
                    j = it % 4
                    hcq = (it // 4) % 32
                    gpsimd.wait_ge(S["fe"].h, fe_stt(it))
                    SQS.issue(gpsimd, it,
                              d_sq[quad * 512 + j * 128:quad * 512 + (j + 1) * 128,
                                   hcq * 256:(hcq + 1) * 256],
                              fsqv(it % 8))

            def f_gp(gpsimd, sc=sc):
                for dt in range(DHT):
                    gpsimd.wait_ge(S["yev"].h, b_yev + dt + 1)
                    SY.issue(gpsimd, 0, d_yin[sc].ap()[dt * 128:(dt + 1) * 128, :],
                             bsl(BA_Y16, 2048))
                    if do_h1 and dt >= 1:
                        sqs_chunk(gpsimd, dt - 1)
                if do_h1:
                    sqs_chunk(gpsimd, DHT - 1)
                SY.wait_all(gpsimd)
                gpsimd.collective_compute(
                    "AllGather", AL.bypass,
                    replica_groups=[[0, 1], [2, 3], [4, 5], [6, 7]],
                    ins=[d_yin[sc].ap().opt()],
                    outs=[d_yout[sc].ap().opt()],
                ).then_inc(S["ag"].h, 1)
                S["ag"].v += 1

            sec("gpsimd", f_gp)
            fence()
            if "y" in tap:
                def f(gpsimd, sc=sc):
                    STP.issue(gpsimd, 0,
                              tap["y"][:, sc * TSC:(sc + 1) * TSC].rearrange("(a p) c -> a p c", p=128),
                              d_yin[sc].ap().rearrange("(a p) c -> a p c", p=128))

                sec("gpsimd", f)
            fence()

        def xnqv(quad):
            base = FA_W if quad == 0 else 16384
            return af[:, base:base + 4096].bitcast(BF16)

        def q1v(p):
            return af[:, p * 2048:(p + 1) * 2048].bitcast(BF16)

        def fh1v(s):
            return af[:, 4096 + s * 128:4096 + (s + 1) * 128].bitcast(BF16)

        def fsqv(s):
            return af[:, 8192 + s * 128:8192 + (s + 1) * 128].bitcast(BF16)

        fe_mark = {}

        def ph_ffn_stats(sc):
            se_xfree = []
            for tb in range(8):
                quad, j = tb // 4, tb % 4
                gate = se_xfree[tb - 2] if tb >= 2 else None

                def f(sync, sc=sc, tb=tb, gate=gate):
                    if gate is not None:
                        sync.wait_ge(S["se"].h, gate)
                    LX.issue(sync, tb, fsl(FA_X + (tb % 2) * 2048, 2048),
                             I["xq"][sc * QT + tb * 128: sc * QT + (tb + 1) * 128, :])

                sec("sync", f)
                tok = (tb % 2, LX.sems[tb % 2].v)

                def f(scalar, tok=tok):
                    LX.wait(scalar, tok)

                sec("scalar", f)
                ln_stats(FA_X + (tb % 2) * 2048, 2048, 8, INVD)

                def f(scalar, tb=tb):
                    scalar.activation(stat[:, 48 + tb:49 + tb], stat[:, 8:9], AF.Copy)
                    scalar.activation(stat[:, 56 + tb:57 + tb], stat[:, 9:10], AF.Copy)
                    S["se"].inc(scalar.activation(fsl(FA_XN, 2048),
                                                  fsl(FA_X + (tb % 2) * 2048, 2048),
                                                  AF.Identity, scale=stat[:, 8:9],
                                                  bias=stat[:, 9:10]))

                sec("scalar", f)
                se_xfree.append(S["se"].v)
                s1 = S["se"].v

                def f(vector, s1=s1):
                    vector.wait_ge(S["se"].h, s1)
                    S["ve"].inc(vector.tensor_copy(bsl(BA_PRE, 2048), fsl(FA_XN, 2048)))

                sec("vector", f)
                transpose16(BA_PRE, BA_DA)
                v0 = S["ve"].v

                def f(vector, quad=quad, j=j, v0=v0):
                    vector.wait_ge(S["ve"].h, v0)
                    for dt in range(DT):
                        S["ve"].inc(vector.tensor_copy(
                            xnqv(quad)[:, dt * 512 + j * 128: dt * 512 + j * 128 + 128],
                            bsl(BA_DA + dt * 128, 128)))

                sec("vector", f)

        def ph_ffn_tail(sc):
            for quad in range(2):
                kb = 16 if quad == 0 else 8
                femk = fe_mark[(sc, quad)]

                def f(vector, sc=sc, quad=quad, kb=kb, femk=femk):
                    vector.wait_ge(S["fe"].h, femk)
                    for j in range(4):
                        ck(vector, "ve", vector.tensor_scalar(stat[:, 12:13], stat[:, kb + j:kb + j + 1],
                                                              sc8[:, 4:5], EPS, AL.mult, AL.max))
                        ck(vector, "ve", vector.reciprocal(stat[:, 13:14], stat[:, 12:13]))
                        vector.tensor_scalar(stat[:, 13:14], stat[:, 13:14],
                                             sc8[:, 4:5], 255.0, AL.mult, AL.mult)
                        ck(vector, "ve", vector.tensor_tensor(stat[:, 14:15], stat[:, 12:13],
                                                              sc8[:, 5:6], AL.mult))
                        vector.tensor_copy(stat[:, 24 + j:25 + j], stat[:, 13:14])
                        S["ve"].inc(vector.tensor_copy(stat[:, 28 + j:29 + j], stat[:, 14:15]))

                sec("vector", f)
                v_k = S["ve"].v

                def f(gpsimd, sc=sc, quad=quad, v_k=v_k):
                    gpsimd.wait_ge(S["ve"].h, v_k)
                    for j in range(4):
                        SPK.issue(gpsimd, 0,
                                  d_pk[sc, (quad * 4 + j) * 128:(quad * 4 + j + 1) * 128]
                                  .rearrange("(p o) -> p o", o=1),
                                  stat[:, 28 + j:29 + j])

                sec("gpsimd", f)

                ve_round = []
                b_lsq = [LSQ.sems[0].v, LSQ.sems[1].v]
                for j in range(4):
                    for ch in range(4):
                        it = j * 4 + ch
                        gate = ve_round[it - 2] if it >= 2 else None

                        def f(sync, sc=sc, quad=quad, j=j, ch=ch, it=it, gate=gate):
                            if it == 0:
                                SQS.wait_all(sync)
                                sync.wait_ge(S["ve"].h, S["ve"].v)
                                sync.wait_ge(S["fe"].h, S["fe"].v)
                            if gate is not None:
                                sync.wait_ge(S["ve"].h, gate)
                            LSQ.issue(sync, it, bsl(BA_SQR + (it % 2) * 2048, 2048),
                                      d_sq[quad * 512 + j * 128:quad * 512 + (j + 1) * 128,
                                           ch * 2048:(ch + 1) * 2048])

                        sec("sync", f)
                        tok = (it % 2, LSQ.sems[it % 2].v)

                        def f(vector, j=j, it=it, tok=tok):
                            LSQ.wait(vector, tok)
                            ck(vector, "ve", vector.tensor_scalar(bsl(BA_SQR + (it % 2) * 2048, 2048),
                                                                  bsl(BA_SQR + (it % 2) * 2048, 2048),
                                                                  stat[:, 24 + j:25 + j], None, AL.mult))
                            S["ve"].inc(vector.tensor_scalar(bsl(BA_SQR + (it % 2) * 2048, 2048),
                                                             bsl(BA_SQR + (it % 2) * 2048, 2048),
                                                             MAGIC, MAGIC, AL.add, AL.subtract))

                        sec("vector", f)
                        for r0 in (0, 8):
                            ve0 = S["ve"].v

                            def f_pe(tensor, it=it, r0=r0, ve0=ve0):
                                tensor.wait_ge(S["ve"].h, ve0)
                                for b in range(8):
                                    tr = tensor.transpose(ps[b][:, 0:64].bitcast(BF16),
                                                          bsl(BA_SQR + (it % 2) * 2048 + (r0 + b) * 128, 128),
                                                          ident16[:])
                                S["pe"].inc(tr)

                            sec("tensor", f_pe)
                            pe0 = S["pe"].v

                            def f_ve(vector, j=j, ch=ch, r0=r0, pe0=pe0):
                                vector.wait_ge(S["pe"].h, pe0)
                                for b in range(8):
                                    K = ch * 16 + r0 + b
                                    S["ve"].inc(vector.tensor_copy(
                                        ab[:, BA_HQT + K * 512 + j * 128: BA_HQT + K * 512 + (j + 1) * 128],
                                        ps[b][:, 0:64].bitcast(BF16)))

                            sec("vector", f_ve)
                        ve_round.append(S["ve"].v)

                b_f2 = S["dcp"].v
                lp0 = LPK.sems[0].v

                def f(sync, sc=sc, quad=quad):
                    SPK.wait_all(sync)
                    LPK.issue(sync, 0, fsl(FA_PK, 512),
                              d_pk[sc, quad * 512:(quad + 1) * 512].partition_broadcast(128))

                sec("sync", f)
                lptok = (0, LPK.sems[0].v)
                vhq = S["ve"].v
                def q2buf(p):
                    return (bsl(0, 8192) if p == 0
                            else af[:, FA_W:FA_W + 4096].bitcast(BF16))

                for dt in range(DT):
                    def f(sync, sc=sc, quad=quad, dt=dt, b_f2=b_f2, vhq=vhq):
                        if dt >= 2:
                            sync.wait_ge(S["dcp"].h, b_f2 + dt - 1)
                        else:
                            sync.wait_ge(S["ve"].h, vhq)
                        LQ2.issue(sync, dt, q2buf(dt % 2)
                                  .rearrange("p (a c) -> p a c", c=128),
                                  d_q2T[:, dt * 128:(dt + 1) * 128]
                                  .rearrange("(a p) c -> p a c", p=128))

                    sec("sync", f)
                    tok = (dt % 2, LQ2.sems[dt % 2].v)

                    def f_pe(tensor, dt=dt, tok=tok, vhq=vhq):
                        LQ2.wait(tensor, tok)
                        tensor.wait_ge(S["ve"].h, vhq)
                        for K in range(64):
                            mm = tensor.matmul(ps[4 + dt % 2][:, 0:512],
                                               q2buf(dt % 2)[:, K * 128:(K + 1) * 128],
                                               bsl(BA_HQT + K * 512, 512),
                                               start=(K == 0), stop=(K == 63))
                        S["pe"].inc(mm)

                    sec("tensor", f_pe)
                    pe0 = S["pe"].v

                    def f(vector, dt=dt, pe0=pe0, lptok=lptok, b_f2=b_f2):
                        vector.wait_ge(S["pe"].h, pe0)
                        LPK.wait(vector, lptok)
                        if dt >= 1:
                            vector.wait_ge(S["dcp"].h, b_f2 + dt)
                        S["ve"].inc(vector.tensor_tensor(fsl(FA_FT, 512), ps[4 + dt % 2][:, 0:512],
                                                         fsl(FA_PK, 512), AL.mult))

                    sec("vector", f)
                    v1 = S["ve"].v
                    sg_tok = (0, SG.sems[0].v)

                    def f(scalar, dt=dt, v1=v1, sg_tok=sg_tok):
                        scalar.wait_ge(S["ve"].h, v1)
                        SG.wait(scalar, sg_tok)
                        scalar.activation(bsl(BA_SILO, 512), fsl(FA_FT, 512), AF.Silu
                                          ).then_inc(S["dcp"].h, 1)

                    sec("scalar", f)
                    S["dcp"].v += 1
                    sd = S["dcp"].v

                    def f(gpsimd, sc=sc, quad=quad, dt=dt, sd=sd):
                        gpsimd.wait_ge(S["dcp"].h, sd)
                        SG.issue(gpsimd, 0, d_g[sc, dt, :, quad * 512:(quad + 1) * 512],
                                 bsl(BA_SILO, 512))

                    sec("gpsimd", f)
            fence()
            if "g" in tap:
                def f(gpsimd, sc=sc):
                    SG.wait_all(gpsimd)
                    STP.issue(gpsimd, 0,
                              tap["g"][:, sc * QT:(sc + 1) * QT].rearrange("(a p) c -> a p c", p=128),
                              d_g[sc, :, :, :])

                sec("gpsimd", f)
            fence()

        def ph_post(sc):
            se_xfree = []
            ve_xnread = []
            v_entry = S["ve"].v
            for tb in range(8):
                gate = se_xfree[tb - 2] if tb >= 2 else None
                vgate = ve_xnread[tb - 1] if tb >= 1 else v_entry

                def f(sync, sc=sc, tb=tb, gate=gate):
                    if gate is not None:
                        sync.wait_ge(S["se"].h, gate)
                    LX.issue(sync, tb, fsl(FA_X + (tb % 2) * 2048, 2048),
                             I["xq"][sc * QT + tb * 128: sc * QT + (tb + 1) * 128, :])

                sec("sync", f)
                tok = (tb % 2, LX.sems[tb % 2].v)

                def f(scalar, tb=tb, tok=tok, vgate=vgate):
                    LX.wait(scalar, tok)
                    scalar.wait_ge(S["ve"].h, vgate)
                    S["se"].inc(scalar.activation(fsl(FA_XN, 2048), fsl(FA_X + (tb % 2) * 2048, 2048),
                                                  AF.Identity, scale=stat[:, 48 + tb:49 + tb],
                                                  bias=stat[:, 56 + tb:57 + tb]))

                sec("scalar", f)
                se_xfree.append(S["se"].v)
                s0 = S["se"].v

                def f(vector, s0=s0):
                    vector.wait_ge(S["se"].h, s0)
                    S["ve"].inc(vector.tensor_copy(bsl(BA_PRE, 2048), fsl(FA_XN, 2048)))

                sec("vector", f)
                ve_xnread.append(S["ve"].v)
                transpose16(BA_PRE, BA_DA)
                v1 = S["ve"].v

                def f(vector, tb=tb, v1=v1):
                    vector.wait_ge(S["ve"].h, v1)
                    for dt in range(DT):
                        S["ve"].inc(vector.tensor_copy(
                            ab[:, BA_V + dt * 1024 + tb * 128: BA_V + dt * 1024 + (tb + 1) * 128],
                            bsl(BA_DA + dt * 128, 128)))

                sec("vector", f)

            agw = S["ag"].v
            v_p1 = S["ve"].v

            def f(sync, agw=agw, v_p1=v_p1):
                sync.wait_ge(S["ag"].h, agw)
                sync.wait_ge(S["ve"].h, v_p1)

            sec("sync", f)
            blend_done = []
            for dt in range(DT):
                gate = blend_done[dt - 2] if dt >= 2 else None

                def f(sync, sc=sc, dt=dt, gate=gate):
                    if gate is not None:
                        sync.wait_ge(S["ve"].h, gate)
                    LY.issue(sync, dt, ab[:, BA_Y01 + (dt % 2) * 2048:BA_Y01 + (dt % 2) * 2048 + 1024],
                             d_yout[sc].ap()[dt * 128:(dt + 1) * 128, 0:QT])
                    LY.sems[dt % 2].inc(sync.dma_start(
                        out=ab[:, BA_Y01 + (dt % 2) * 2048 + 1024:BA_Y01 + (dt % 2) * 2048 + 2048],
                        in_=d_yout[sc].ap()[dt * 128:(dt + 1) * 128, QT:TSC]), 16)

                sec("sync", f)
                tok = (dt % 2, LY.sems[dt % 2].v)

                def f(vector, dt=dt, tok=tok):
                    LY.wait(vector, tok)
                    y0 = ab[:, BA_Y01 + (dt % 2) * 2048:BA_Y01 + (dt % 2) * 2048 + 1024]
                    y1 = ab[:, BA_Y01 + (dt % 2) * 2048 + 1024:BA_Y01 + (dt % 2) * 2048 + 2048]
                    vv = ab[:, BA_V + dt * 1024:BA_V + (dt + 1) * 1024]
                    vector.tensor_scalar(y0, y0, msel_t[:, 0:1], None, AL.mult)
                    ck(vector, "ve", vector.tensor_scalar(y1, y1, msel_t[:, 1:2], None, AL.mult))
                    vector.tensor_add(y0, y0, y1)
                    ck(vector, "ve", vector.tensor_scalar(vv, vv, ccol[:, 32 + dt:33 + dt],
                                                          None, AL.mult))
                    S["ve"].inc(vector.tensor_add(vv, vv, y0))

                sec("vector", f)
                blend_done.append(S["ve"].v)
            v_v = S["ve"].v
            if "vtap" in tap:
                def f(gpsimd, sc=sc, v_v=v_v):
                    gpsimd.wait_ge(S["ve"].h, v_v)
                    STP.issue(gpsimd, 0,
                              tap["vtap"][:, sc * QT:(sc + 1) * QT].rearrange("(a p) c -> p a c", p=128),
                              ab[:, BA_V:BA_V + 16384].rearrange("p (a c) -> p a c", c=1024))

                sec("gpsimd", f)

            fin_done = []
            for ot in range(16):
                wgate = fin_done[ot - 2] if ot >= 2 else None

                def f(sync, sc=sc, ot=ot, v_v=v_v, wgate=wgate):
                    sync.wait_ge(S["ve"].h, v_v if wgate is None else wgate)
                    LWO.issue(sync, ot, bsl(BA_WOS + (ot % 2) * 2048, 2048)
                              .rearrange("p (a c) -> p a c", c=128),
                              d_woT[ot, :, :, :].rearrange("a p c -> p a c"))
                    LG.issue(sync, ot, ab[:, BA_GP + (ot % 2) * 1024:BA_GP + (ot % 2) * 1024 + 1024],
                             d_g[sc, ot, :, :])

                sec("sync", f)
                tokw = (ot % 2, LWO.sems[ot % 2].v)
                tokg = (ot % 2, LG.sems[ot % 2].v)
                vfin = S["ve"].v

                def f_pe(tensor, ot=ot, tokw=tokw, vfin=vfin):
                    LWO.wait(tensor, tokw)
                    tensor.wait_ge(S["ve"].h, vfin)
                    for tc in range(2):
                        for K in range(16):
                            mm = tensor.matmul(ps[4 + 2 * (ot % 2) + tc][:, 0:512],
                                               bsl(BA_WOS + (ot % 2) * 2048 + K * 128, 128),
                                               ab[:, BA_V + K * 1024 + tc * 512:BA_V + K * 1024 + tc * 512 + 512],
                                               start=(K == 0), stop=(K == 15))
                    S["pe"].inc(mm)

                sec("tensor", f_pe)
                pe0 = S["pe"].v

                def f(vector, ot=ot, pe0=pe0, tokg=tokg):
                    vector.wait_ge(S["pe"].h, pe0)
                    LG.wait(vector, tokg)
                    for tc in range(2):
                        cp = vector.tensor_copy(ab[:, BA_SOUT + tc * 512:BA_SOUT + (tc + 1) * 512],
                                                ps[4 + 2 * (ot % 2) + tc][:, 0:512])
                    ck(vector, "ve", cp)
                    S["ve"].inc(vector.tensor_tensor(
                        ab[:, BA_FIN + ot * 1024:BA_FIN + (ot + 1) * 1024],
                        ab[:, BA_SOUT:BA_SOUT + 1024],
                        ab[:, BA_GP + (ot % 2) * 1024:BA_GP + (ot % 2) * 1024 + 1024], AL.mult))

                sec("vector", f)
                fin_done.append(S["ve"].v)
            v_fin = S["ve"].v
            if "sout" in tap:
                def f(gpsimd, sc=sc, v_fin=v_fin):
                    gpsimd.wait_ge(S["ve"].h, v_fin)
                    STP.issue(gpsimd, 0,
                              tap["sout"][:, sc * QT:(sc + 1) * QT].rearrange("(a p) c -> p a c", p=128),
                              ab[:, BA_FIN:BA_FIN + 16384].rearrange("p (a c) -> p a c", c=1024))

                sec("gpsimd", f)

            add_done = []
            for tb in range(8):
                vstage = S["ve"].v if tb == 0 else None

                BA_FSTG = 34816

                def f(vector, tb=tb, v_fin=v_fin, BA_FSTG=BA_FSTG):
                    if tb == 0:
                        vector.wait_ge(S["ve"].h, v_fin)
                    for dt in range(DT):
                        S["ve"].inc(vector.tensor_copy(
                            bsl(BA_FSTG + dt * 128, 128),
                            ab[:, BA_FIN + dt * 1024 + tb * 128:BA_FIN + dt * 1024 + (tb + 1) * 128]))

                sec("vector", f)
                transpose16(BA_FSTG, BA_DA)
                v1 = S["ve"].v
                so_tok = (0, SO.sems[0].v)
                gate = add_done[tb - 2] if tb >= 2 else None

                def f(sync, sc=sc, tb=tb, gate=gate):
                    if gate is not None:
                        sync.wait_ge(S["ve"].h, gate)
                    LX.issue(sync, tb, fsl(FA_X + (tb % 2) * 2048, 2048),
                             I["xq"][sc * QT + tb * 128: sc * QT + (tb + 1) * 128, :])

                sec("sync", f)
                tok = (tb % 2, LX.sems[tb % 2].v)

                def f(vector, tb=tb, v1=v1, tok=tok, so_tok=so_tok):
                    vector.wait_ge(S["ve"].h, v1)
                    LX.wait(vector, tok)
                    SO.wait(vector, so_tok)
                    S["ve"].inc(vector.tensor_add(fsl(FA_SQ, 2048), fsl(FA_X + (tb % 2) * 2048, 2048),
                                                  bsl(BA_DA, 2048)))

                sec("vector", f)
                add_done.append(S["ve"].v)
                v2 = S["ve"].v

                def f(sync, sc=sc, tb=tb, v2=v2):
                    sync.wait_ge(S["ve"].h, v2)
                    SO.issue(sync, 0, out_o[sc * QT + tb * 128: sc * QT + (tb + 1) * 128, :],
                             fsl(FA_SQ, 2048))

                sec("sync", f)
            fence()

        fence()
        for sc in range(2):
            if "pre" in phases:
                ph_pre(sc)
            if "ffn" in phases:
                ph_ffn_stats(sc)
                fence()
            if "scan" in phases:
                ph_scan(sc)
            if "ffn" in phases:
                ph_ffn_tail(sc)
            if "post" in phases:
                ph_post(sc)

        def f(sync):
            for st in [SY, SB, SQS, SQW, SSC, SG, SO, SPK, STP]:
                st.wait_all(sync)
            sync.wait_ge(S["ag"].h, S["ag"].v)

        sec("sync", f)

    return nc, tap


def _shard_inputs(inputs):
    x = inputs["L0_attn_x"]
    per_core = []
    for c in range(8):
        b, dh = c // 2, c % 2
        xb = np.ascontiguousarray(x[b])
        xq = np.ascontiguousarray(
            np.concatenate([xb[dh * QT:(dh + 1) * QT], xb[TSC + dh * QT:TSC + (dh + 1) * QT]], axis=0))
        dsl = slice(dh * (D // 2), (dh + 1) * (D // 2))
        m = np.zeros((1, 2), np.float32)
        m[0, dh] = 1.0
        per_core.append({
            "xsc": xb,
            "xdh": np.ascontiguousarray(xb[:, dsl]),
            "xq": xq,
            "nw": inputs["L0_attn_norm_w"], "nb": inputs["L0_attn_norm_b"],
            "dprm": inputs["L0_attn_D_param"],
            "nwdh": np.ascontiguousarray(inputs["L0_attn_norm_w"][dsl]),
            "nbdh": np.ascontiguousarray(inputs["L0_attn_norm_b"][dsl]),
            "alogdh": np.ascontiguousarray(inputs["L0_attn_A_log"][dsl]),
            "dpwdh": np.ascontiguousarray(inputs["L0_attn_dt_proj_w"][dsl]),
            "dpbdh": np.ascontiguousarray(inputs["L0_attn_dt_proj_b"][dsl]),
            "xpw": inputs["L0_attn_x_proj_w"],
            "wout": inputs["L0_attn_out_proj_w"],
            "cfc": inputs["L0_ffn_c_fc"], "cpr": inputs["L0_ffn_c_proj"],
            "msel": m,
        })
    return per_core


def kernel(**inputs):
    nc, _ = build_program()
    per_core = _shard_inputs(inputs)
    res = run_bass_kernel_spmd(nc, per_core, core_ids=list(range(8)))
    x = inputs["L0_attn_x"]
    out = np.empty_like(x)
    for c in range(8):
        b, dh = c // 2, c % 2
        o = np.asarray(res.results[c]["o"])
        out[b, dh * QT:(dh + 1) * QT] = o[:QT]
        out[b, TSC + dh * QT:TSC + (dh + 1) * QT] = o[QT:]
    return out
